# version 19
# baseline (speedup 1.0000x reference)
import contextlib
import numpy as np
import ml_dtypes
import concourse.bass as bass
import concourse.mybir as mybir
from concourse.bass_utils import run_bass_kernel_spmd

F32 = mybir.dt.float32
BF16 = mybir.dt.bfloat16
I32 = mybir.dt.int32
AF = mybir.ActivationFunctionType
ALU = mybir.AluOpType
AX = mybir.AxisListType

B, S, D = 2, 8192, 1024
T = 2048
NT = 16
KC = 8
H, HD, CS = 16, 64, 16
NCH = T // CS
HEADS, DH, WIN = 8, 64, 512
INNER = 512
SPAN = 640
HID = 2730
HIDP = 2816
KH = 22
EPS = 1e-5
AG1N = 16 + 1024
AGKN = 8 * 128 * 512 + 1024
AGCN = 1024
WSL = 512
W1GW = [512, 512, 512, 512, 512, 170]
W1G0 = [0, 512, 1024, 1536, 2048, 2560]
NW1G = 6
AR = 21576


def _f(x):
    return np.ascontiguousarray(np.asarray(x), dtype=np.float32)


def _bfn(x):
    return np.ascontiguousarray(np.asarray(x, dtype=np.float32).astype(ml_dtypes.bfloat16))


class Ct:
    def __init__(self, sem, step=1):
        self.sem, self.step, self.n = sem, step, 0

    def inc(self, instr):
        instr.then_inc(self.sem, self.step)
        self.n += self.step

    def wait(self, eng):
        if self.n > 0:
            eng.wait_ge(self.sem, self.n)


def pbc(row_ap, n):
    dims = [[1, 1], [0, n]] + [list(d) for d in row_ap.ap[1:]]
    return bass.AP(row_ap.tensor, row_ap.offset, dims)


def rotread(ap2d, nh, hd):
    ps = ap2d.ap[0][0]
    half = hd // 2
    return bass.AP(ap2d.tensor, ap2d.offset + half,
                   [[ps, ap2d.ap[0][1]], [hd, nh], [-half, 2], [1, half]])


def build_program(dbg=()):
    nc = bass.Bass("TRN2", target_bir_lowering=False)
    dbg = set(dbg)
    taps = []

    def din(name, shape, dt=F32):
        return nc.dram_tensor(name, shape, dt, kind="ExternalInput")

    xin = din("xin", [T, D])
    w_inT = din("w_inT", [D, D])
    w_gateT = din("w_gateT", [D, 2 * H])
    a_ln_g = din("a_ln_g", [D])
    a_ln_b = din("a_ln_b", [D])
    bgate = din("bgate", [2 * H])
    eigr = din("eigr", [H])
    w_outT = din("w_outT", [D, D])
    wqkv = din("wqkv", [2, D, 3 * INNER])
    wo = din("wo", [2, INNER, D])
    w1 = din("w1", [2, D, 2 * HID])
    b1 = din("b1", [2, 2 * HID])
    w2p = din("w2p", [2, HIDP, D])
    b2 = din("b2", [2, D])
    f_ln_g = din("f_ln_g", [2, D])
    f_ln_b = din("f_ln_b", [2, D])
    qkpat = din("qkpat", [2, 128, 256])
    rotpk = din("rotpk", [T, 256])
    maskt = din("maskt", [NT, 128, SPAN], BF16)
    identb = din("identb", [128, 128], BF16)
    onescol = din("onescol", [128, 1], BF16)
    onesrow = din("onesrow", [1, 128], BF16)
    estrip = din("estrip", [KC, 128, H], BF16)
    selAd = din("selAd", [H, 8])
    selAid = din("selAid", [H, 8])
    selBd = din("selBd", [H, 8 * HD])
    qzin = din("qzin", [128, 1])
    poffk = din("poffk", [1, 1], I32)
    poffc = din("poffc", [1, 1], I32)

    out = nc.dram_tensor("out", [T, D], F32, kind="ExternalOutput")
    x1sp = nc.dram_tensor("x1sp", [T, D], F32)
    cbsp = nc.dram_tensor("cbsp", [48, T], F32)
    rwsp = nc.dram_tensor("rwsp", [2, T], F32)
    hgesp = nc.dram_tensor("hgesp", [NT, KH, 128, 128], BF16)
    ag1_in = nc.dram_tensor("ag1_in", [1, AG1N], F32)
    ag1_out = nc.dram_tensor("ag1_out", [8, AG1N], F32)
    agk_in = [nc.dram_tensor(f"agk_in{l}", [1, AGKN], BF16) for l in range(2)]
    agk_out = [nc.dram_tensor(f"agk_out{l}", [8, AGKN], BF16) for l in range(2)]
    agc_in = [nc.dram_tensor(f"agc_in{l}", [1, AGCN], BF16) for l in range(2)]
    agc_out = [nc.dram_tensor(f"agc_out{l}", [8, AGCN], BF16) for l in range(2)]

    est = contextlib.ExitStack()
    with est:
        def sbt(name, shape, dt):
            return est.enter_context(nc.sbuf_tensor(name, shape, dt))

        cur = sbt("cur", [128, NT * D], F32)
        wAr = sbt("wAr", [128, 12288], BF16)
        scr = sbt("scr", [128, 2048], F32)
        cp2 = sbt("cp2", [128, 1536], F32)
        stt = sbt("stt", [128, 80], F32)
        tlb = sbt("tlb", [128, 1024], BF16)
        tlb2 = sbt("tlb2", [128, 1024], BF16)
        dbb = sbt("dbb", [128, 128], BF16)
        ptb = sbt("ptb", [128, SPAN], BF16)
        rtt = sbt("rtt", [128, 256], F32)
        mkt = sbt("mkt", [128, SPAN], BF16)
        zzb = sbt("zzb", [128, 128], BF16)
        sid = sbt("sid", [128, 128], BF16)
        so_c = sbt("so_c", [128, 1], BF16)
        so_r = sbt("so_r", [1, 128], BF16)
        sest = sbt("sest", [128, KC * H], BF16)
        cmat = sbt("cmat", [128, H], BF16)
        lngA = sbt("lngA", [128, KC], F32)
        lnbA = sbt("lnbA", [128, KC], F32)
        lngF = sbt("lngF", [128, KC], F32)
        lnbF = sbt("lnbF", [128, KC], F32)
        lnb_b = sbt("lnb_b", [128, KC], BF16)
        lnb_lo = sbt("lnb_lo", [128, KC], BF16)
        qkp = sbt("qkp", [128, 256], F32)
        sqz = sbt("sqz", [128, 1], F32)
        svc = sbt("svc", [16, D], BF16)
        pcol = sbt("pcol", [128, KC], BF16)
        sb1g = sbt("sb1g", [1, 1536], BF16)
        sb2r = sbt("sb2r", [1, D], BF16)
        soffk = sbt("soffk", [1, 1], I32)
        soffc = sbt("soffc", [1, 1], I32)
        seps = sbt("seps", [128, 1], F32)
        pib = sbt("pib", [16, 2048], BF16)
        arena = sbt("arena", [128, AR], F32)

        def arf(off, n, p0=0, p1=128):
            assert off + n <= AR, (off, n)
            return arena[p0:p1, off:off + n]

        def arb(off, n, p0=0, p1=128):
            assert off + (n + 1) // 2 <= AR, (off, n)
            return arena[p0:p1, off:off + (n + 1) // 2].bitcast(BF16)

        stripA = [arb(k * 1025, 1 + T) for k in range(KC)]
        sigv = arf(0, 2048, 0, 32)
        acmp = arf(0, 2048, 0, 16)
        betav = arf(0, 2048, 16, 32)
        cumA = arf(2048, 2048, 0, 16)
        cumB = arf(2048 + 2048, 2048, 0, 16)
        aindv = arf(6144, 2048, 0, 16)
        stripB = [arb(8200 + k * 1024, T) for k in range(KC)]
        kts = [arb(8200 + j * 1280, WIN + T) for j in range(4)]
        vts = arb(13320, (NT + 4) * INNER)
        BC1 = arf(16392, 1024)
        BFz = arf(17416, 1024)
        BC2 = arf(18440, 1024)
        Qs = arf(19464, 1024)
        Vs = arf(20488, 1024)
        qtt = arb(18440, 512)
        qtt5 = arb(20296, 5 * 512)
        qtt = arb(18440, 512)
        qtt5 = arb(18696, 5 * 512)
        loTt = arb(18440, 1024)
        graw = arf(16392, 2048, 0, 32)
        wg32 = arf(19464, 256)
        wg_hi = arb(19720, 256)
        wg_lo = arb(19848, 256)
        smask = arf(19976, 1280)
        pbuf = arb(21256, SPAN)
        pbuf2 = arb(19976, SPAN)
        dbb2 = arb(20296, 128)
        w2h = arb(8200, KH * 512)
        sqs = arb(13832, 1 + T)
        bcr1 = arf(14857, 1024)
        bcr2 = arf(15881, 1024)
        hgt = arb(16905, HIDP)
        bscr = arf(18313, 1024)

        CP_SELB = 0
        CP_BALL = 512
        CP_SELA = 1024
        CP_SELAI = 1032
        CP_EIG = 1040
        CP_BG = 1041
        CP_C2 = 1042
        CP_AALL = 1043
        CP_R = 1051
        CP_CST = 1115
        CP_PUB = 1123
        CP_CHA = 1124
        CP_CHB = 1252

        def pst(name, shape, dt):
            return est.enter_context(nc.psum_tensor(name, shape, dt))

        PTR = pst("PTR", [128, 1024], BF16)
        PQ = pst("PQ", [128, 1536], F32)
        PS0 = pst("PS0", [128, 640], F32)
        PS1 = pst("PS1", [128, 640], F32)

        sd = Ct(est.enter_context(nc.semaphore("sd")), 16)
        sg = Ct(est.enter_context(nc.semaphore("sg")), 16)
        sv = Ct(est.enter_context(nc.semaphore("sv")), 1)
        ss = Ct(est.enter_context(nc.semaphore("ss")), 1)
        stn = Ct(est.enter_context(nc.semaphore("stn")), 1)
        sq = Ct(est.enter_context(nc.semaphore("sq")), 1)
        scc = Ct(est.enter_context(nc.semaphore("scc")), 1)
        fin = Ct(est.enter_context(nc.semaphore("fin")), 16)
        ALLC = [sd, sg, sv, ss, stn, sq, scc]

        def VI(v, instr):
            sv.inc(instr)
            v.wait_ge(sv.sem, sv.n)

        def SI(sc, instr):
            ss.inc(instr)
            sc.wait_ge(ss.sem, ss.n)

        _BQ = []

        def flushb():
            if not _BQ:
                return
            with nc.Block(no_gpsimd_drain=True) as b:
                for en, fn in _BQ:
                    getattr(b, en)(fn)
            _BQ.clear()

        def blk(*fns):
            _BQ.extend(fns)
            if len(_BQ) >= 48:
                flushb()

        def wall(eng):
            for c in ALLC:
                c.wait(eng)

        def tap(name, apf, shape, dt=F32):
            if name not in dbg:
                return
            dt_ = nc.dram_tensor("dbg_" + name, shape, dt, kind="ExternalOutput")

            def _t(s):
                wall(s)
                sd.inc(s.dma_start(out=dt_.ap(), in_=apf()))
            blk(("sync", _t))

            def _w(e):
                sd.wait(e)
            blk(("vector", _w), ("scalar", _w), ("tensor", _w), ("gpsimd", _w))
            taps.append("dbg_" + name)

        wv = lambda a, b_: wAr[:, a:b_]
        wvin = lambda k: wv(k * 1024, (k + 1) * 1024)
        wvg = lambda k: wv(8192 + k * 32, 8192 + (k + 1) * 32)
        wvq = lambda k: wv(k * 1536, (k + 1) * 1536)
        wvo = lambda j: wv(j * 1024, (j + 1) * 1024)

        def s_setup(s):
            dst = cur[:, 0:8 * D].rearrange("p (i c) -> p i c", c=D)
            src = bass.AP(xin, 0, [[D, 128], [128 * D, 8], [1, D]])
            sd.inc(s.dma_start(out=dst, in_=src))
            sd.inc(s.dma_start(out=sid[:], in_=identb.ap()))
            sd.inc(s.dma_start(out=so_c[:], in_=onescol.ap()))
            sd.inc(s.dma_start(out=so_r[:], in_=onesrow.ap()))
            sd.inc(s.dma_start(out=sest[:].rearrange("p (k h) -> p k h", h=H),
                               in_=bass.AP(estrip, 0, [[H, 128], [128 * H, KC], [1, H]])))
            sd.inc(s.dma_start(out=lngA[:], in_=bass.AP(a_ln_g, 0, [[1, 128], [128, KC]]), allow_slow_non_contiguous=True))
            sd.inc(s.dma_start(out=lnbA[:], in_=bass.AP(a_ln_b, 0, [[1, 128], [128, KC]]), allow_slow_non_contiguous=True))
            sd.inc(s.dma_start(out=sqz[:], in_=qzin.ap()))
            sd.inc(s.dma_start(out=soffk[:], in_=poffk.ap()))
            sd.inc(s.dma_start(out=soffc[:], in_=poffc.ap()))
            sd.inc(s.dma_start(out=cp2[0:32, CP_BG:CP_BG + 1],
                               in_=bass.AP(bgate, 0, [[1, 32], [1, 1]])))
            sd.inc(s.dma_start(
                out=wg32[:].rearrange("p (k c) -> p k c", c=2 * H),
                in_=bass.AP(w_gateT, 0, [[2 * H, 128], [128 * 2 * H, KC],
                                         [1, 2 * H]])))
            sd.inc(s.dma_start(out=cp2[0:16, CP_EIG:CP_EIG + 1],
                               in_=bass.AP(eigr, 0, [[1, 16], [1, 1]])))
            sd.inc(s.dma_start(out=cp2[0:16, CP_SELA:CP_SELA + 8], in_=selAd.ap()))
            sd.inc(s.dma_start(out=cp2[0:16, CP_SELAI:CP_SELAI + 8], in_=selAid.ap()))
            sd.inc(s.dma_start(out=cp2[0:16, CP_SELB:CP_SELB + 512], in_=selBd.ap()))

        def s_setup_g(g):
            dst2 = cur[:, 8 * D:16 * D].rearrange("p (i c) -> p i c", c=D)
            src2 = bass.AP(xin, 8 * 128 * D, [[D, 128], [128 * D, 8], [1, D]])
            sg.inc(g.dma_start(out=dst2, in_=src2))
            sg.inc(g.dma_start(out=wv(0, 8192).rearrange("p (k c) -> p k c", c=D),
                               in_=bass.AP(w_inT, 0, [[D, 128], [128 * D, KC], [1, D]])))
            sg.inc(g.dma_start(
                out=wv(8192, 8448).rearrange("p (k c) -> p k c", c=32),
                in_=bass.AP(w_gateT, 0, [[2 * H, 128], [128 * 2 * H, KC], [1, 2 * H]])))

        blk(("sync", s_setup), ("gpsimd", s_setup_g))

        def s_pre(sc):
            sd.wait(sc)
            SI(sc, sc.activation(cp2[0:16, CP_EIG:CP_EIG + 1],
                                 cp2[0:16, CP_EIG:CP_EIG + 1], AF.Tanh))
            SI(sc, sc.copy(lnb_b[:], lnbA[:]))
            SI(sc, sc.copy(wg_hi[:], wg32[:]))

        def s_pre_v(v):
            VI(v, v.memset(zzb[:], 0.0))
            VI(v, v.memset(seps[:], EPS))
            ss.wait(v)
            VI(v, v.tensor_tensor(lnb_lo[:], lnbA[:], lnb_b[:], ALU.subtract))
            VI(v, v.tensor_tensor(wg_lo[:], wg32[:], wg_hi[:], ALU.subtract))

        blk(("scalar", s_pre), ("vector", s_pre_v))

        def emit_ln_stats():
            def f_v(v):
                sd.wait(v)
                sg.wait(v)
                sv.wait(v)
                for i in range(NT):
                    VI(v, v.reduce_sum(stt[:, i:i + 1], cur[:, i * D:(i + 1) * D],
                                        AX.X))

            def f_s(sc):
                sd.wait(sc)
                sg.wait(sc)
                sv.wait(sc)
                for i in range(NT):
                    SI(sc, sc.activation(scr[:, 0:1024], cur[:, i * D:(i + 1) * D],
                                         AF.Square, accum_out=stt[:, 16 + i:17 + i]))

            blk(("vector", f_v), ("scalar", f_s))

            def f_v1(v):
                ss.wait(v)
                VI(v, v.tensor_scalar_mul(stt[:, 32:48], stt[:, 0:16], 1.0 / D))
                VI(v, v.tensor_tensor(stt[:, 64:80], stt[:, 32:48], stt[:, 32:48],
                                       ALU.mult))
                VI(v, v.tensor_scalar_mul(stt[:, 48:64], stt[:, 16:32], 1.0 / D))
                VI(v, v.tensor_tensor(stt[:, 48:64], stt[:, 48:64], stt[:, 64:80],
                                       ALU.subtract))

            def f_s2(sc):
                sv.wait(sc)
                SI(sc, sc.activation(stt[:, 48:64], stt[:, 48:64], AF.Sqrt, bias=seps[:]))

            def f_v2(v):
                ss.wait(v)
                VI(v, v.reciprocal(stt[:, 48:64], stt[:, 48:64]))

            blk(("vector", f_v1))
            blk(("scalar", f_s2))
            blk(("vector", f_v2))

        def emit_tile_transpose(strips, col0, i, src=None):
            srcf = src or (lambda k: tlb[:, k * 128:(k + 1) * 128])

            def t_f(te):
                sv.wait(te)
                ss.wait(te)
                for k in range(KC):
                    ins = te.transpose(PTR[:, k * 128:(k + 1) * 128], srcf(k), sid[:])
                stn.inc(ins)

            def s_f(v):
                stn.wait(v)
                ins = None
                for k in range(KC):
                    ins = v.tensor_copy(strips[k][:, col0 + i * 128: col0 + (i + 1) * 128],
                                        PTR[:, k * 128:(k + 1) * 128])
                ss.inc(ins)
            blk(("tensor", t_f), ("vector", s_f))

        tap("x0", lambda: cur[:, 0:1024], [128, 1024])
        emit_ln_stats()
        tap("st0", lambda: stt[:], [128, 80])

        for i in range(NT):
            def a2_v(v, i=i):
                ss.wait(v)
                VI(v, v.tensor_scalar(tlb[:], cur[:, i * D:(i + 1) * D],
                                       stt[:, 32 + i:33 + i], stt[:, 48 + i:49 + i],
                                       ALU.subtract, ALU.mult))
            blk(("vector", a2_v))
            if i == 0:
                tap("tl0", lambda: tlb[:], [128, 1024], BF16)
            emit_tile_transpose(stripA, 1, i)

        tap("ln0T", lambda: stripA[0][:, 1:1 + T], [128, T], BF16)

        def a4_t(te):
            ss.wait(te)
            sv.wait(te)
            nmm = 0
            for (lv, wv_) in ((lnb_b, wg_hi), (lnb_b, wg_lo), (lnb_lo, wg_hi)):
                for k in range(KC):
                    ins = te.matmul(PS0[0:1, 0:32], lv[:, k:k + 1],
                                    wv_[:, k * 2 * H:(k + 1) * 2 * H],
                                    start=(nmm == 0), stop=(nmm == 23))
                    nmm += 1
            stn.inc(ins)

        def a4_v(v):
            stn.wait(v)
            VI(v, v.tensor_copy(scr[0:1, 0:32], PS0[0:1, 0:32]))

        blk(("tensor", a4_t), ("vector", a4_v))

        def a4_s(s):
            sv.wait(s)
            sd.inc(s.dma_start(out=cp2[0:32, CP_C2:CP_C2 + 1], in_=scr[0:1, 0:32]))

        blk(("sync", a4_s))

        def a4_v2(v):
            sd.wait(v)
            stn.wait(v)
            VI(v, v.tensor_tensor(cp2[0:32, CP_C2:CP_C2 + 1],
                                   cp2[0:32, CP_C2:CP_C2 + 1],
                                   cp2[0:32, CP_BG:CP_BG + 1], ALU.add))
            for k in range(KC):
                VI(v, v.tensor_scalar_mul(wg32[:, k * 2 * H:(k + 1) * 2 * H],
                                          wg32[:, k * 2 * H:(k + 1) * 2 * H],
                                          lngA[:, k:k + 1]))
            VI(v, v.tensor_copy(wg_hi[:], wg32[:]))
            VI(v, v.tensor_tensor(wg_lo[:], wg32[:], wg_hi[:], ALU.subtract))

        blk(("vector", a4_v2))


        for i in range(NT):
            def a3_t(te, i=i):
                ss.wait(te)
                sg.wait(te)
                sv.wait(te)
                for nb in range(2):
                    for k in range(KC):
                        ins = te.matmul(PQ[:, nb * 512:(nb + 1) * 512],
                                        stripA[k][:, 1 + i * 128: 1 + (i + 1) * 128],
                                        wvin(k)[:, nb * 512:(nb + 1) * 512],
                                        start=(k == 0), stop=(k == KC - 1))
                stn.inc(ins)

            def a3_v(v, i=i):
                stn.wait(v)
                VI(v, v.reduce_sum(stt[:, 0:1], PQ[:, 0:1024], AX.X))
                VI(v, v.tensor_copy(scr[:, 0:1024], PQ[:, 0:1024]))
                VI(v, v.tensor_tensor(scr[:, 0:1024], scr[:, 0:1024],
                                       scr[:, 0:1024], ALU.mult))
                VI(v, v.reduce_sum(stt[:, 1:2], scr[:, 0:1024], AX.X))

            def a3_v2(v, i=i):
                ss.wait(v)
                VI(v, v.tensor_scalar_mul(stt[:, 2:3], stt[:, 0:1], 1.0 / D))
                VI(v, v.tensor_tensor(stt[:, 4:5], stt[:, 2:3], stt[:, 2:3], ALU.mult))
                VI(v, v.tensor_scalar_mul(stt[:, 3:4], stt[:, 1:2], 1.0 / D))
                VI(v, v.tensor_tensor(stt[:, 3:4], stt[:, 3:4], stt[:, 4:5],
                                       ALU.subtract))

            def a3_s2(sc, i=i):
                sv.wait(sc)
                SI(sc, sc.activation(stt[:, 3:4], stt[:, 3:4], AF.Sqrt, bias=seps[:]))

            def a3_v3(v, i=i):
                ss.wait(v)
                VI(v, v.reciprocal(stt[:, 3:4], stt[:, 3:4]))
                VI(v, v.tensor_scalar(scr[:, 1024:2048], PQ[:, 0:1024],
                                       stt[:, 2:3], stt[:, 3:4],
                                       ALU.subtract, ALU.mult))
                VI(v, v.tensor_copy(tlb[:], scr[:, 1024:2048]))
                VI(v, v.tensor_tensor(tlb2[:], scr[:, 1024:2048], tlb[:],
                                       ALU.subtract))

            blk(("tensor", a3_t), ("vector", a3_v))
            blk(("vector", a3_v2))
            blk(("scalar", a3_s2))
            blk(("vector", a3_v3))
            emit_tile_transpose(stripB, 0, i)

            def a3_tlo(te, i=i):
                sv.wait(te)
                ss.wait(te)
                for k in range(KC):
                    ins = te.transpose(PTR[:, k * 128:(k + 1) * 128],
                                       tlb2[:, k * 128:(k + 1) * 128], sid[:])
                stn.inc(ins)

            def a3_slo(v, i=i):
                stn.wait(v)
                ss.inc(v.tensor_copy(loTt[:], PTR[:]))
                v.wait_ge(ss.sem, ss.n)

            blk(("tensor", a3_tlo), ("vector", a3_slo))

            def a3_tg(te, i=i):
                ss.wait(te)
                nmm = 0
                for (lhs_w, rhs_t) in ((wg_hi, 0), (wg_lo, 0), (wg_hi, 1)):
                    for k in range(KC):
                        rhs = (stripB[k][:, i * 128:(i + 1) * 128] if rhs_t == 0
                               else loTt[:, k * 128:(k + 1) * 128])
                        ins = te.matmul(PS1[0:32, 0:128],
                                        lhs_w[:, k * 2 * H:(k + 1) * 2 * H], rhs,
                                        start=(nmm == 0), stop=(nmm == 23))
                        nmm += 1
                stn.inc(ins)

            def a3_sg(v, i=i):
                stn.wait(v)
                ss.inc(v.tensor_copy(graw[:, i * 128:(i + 1) * 128],
                                     PS1[0:32, 0:128]))
                v.wait_ge(ss.sem, ss.n)

            blk(("tensor", a3_tg), ("vector", a3_sg))

        tap("xplT", lambda: stripB[0][:, 0:T], [128, T], BF16)

        for cb in range(4):
            def a5_s(sc, cb=cb):
                ss.wait(sc)
                sv.wait(sc)
                SI(sc, sc.activation(sigv[:, cb * 512:(cb + 1) * 512],
                                     graw[:, cb * 512:(cb + 1) * 512], AF.Sigmoid,
                                     bias=cp2[0:32, CP_C2:CP_C2 + 1]))

            blk(("scalar", a5_s))

        def a5_v(v):
            ss.wait(v)
            VI(v, v.tensor_scalar_mul(acmp[:], acmp[:],
                                       cp2[0:16, CP_EIG:CP_EIG + 1]))

        blk(("vector", a5_v))
        tap("acmp", lambda: acmp[:], [16, T])
        tap("beta", lambda: betav[:], [16, T])

        def a6_v(v):
            sv.wait(v)
            a3_ = acmp.rearrange("p (n i) -> p n i", i=CS)
            c3 = cumA.rearrange("p (n i) -> p n i", i=CS)
            VI(v, v.tensor_copy(c3[:, :, 0:1], a3_[:, :, 0:1]))
            for ii in range(1, CS):
                VI(v, v.tensor_tensor(c3[:, :, ii:ii + 1], c3[:, :, ii - 1:ii],
                                      a3_[:, :, ii:ii + 1], ALU.mult))
            VI(v, v.tensor_tensor(cumB[:], cumA[:], cumA[:], ALU.mult))
            sq3 = cumB.rearrange("p (n i) -> p n i", i=CS)
            i3 = aindv.rearrange("p (n i) -> p n i", i=CS)
            VI(v, v.tensor_single_scalar(i3[:, :, 1:CS], sq3[:, :, 0:CS - 1],
                                          1e-16, ALU.is_gt))
            VI(v, v.memset(i3[:, :, 0:1], 1.0))
            VI(v, v.tensor_tensor(aindv[:], aindv[:], acmp[:], ALU.mult))
            VI(v, v.tensor_copy(cp2[0:16, CP_CHA:CP_CHA + NCH], c3[:, :, CS - 1]))
            for stp, sf in enumerate([1, 2, 4, 8, 16, 32, 64]):
                srcr = cp2[0:16, CP_CHA:CP_CHA + NCH] if stp % 2 == 0 \
                    else cp2[0:16, CP_CHB:CP_CHB + NCH]
                dstr = cp2[0:16, CP_CHB:CP_CHB + NCH] if stp % 2 == 0 \
                    else cp2[0:16, CP_CHA:CP_CHA + NCH]
                VI(v, v.tensor_copy(dstr[:, 0:sf], srcr[:, 0:sf]))
                VI(v, v.tensor_tensor(dstr[:, sf:NCH], srcr[:, sf:NCH],
                                       srcr[:, 0:NCH - sf], ALU.mult))
            VI(v, v.tensor_copy(cumB[:, 0:CS], cumA[:, 0:CS]))
            pe = cp2[0:16, CP_CHB:CP_CHB + NCH - 1].unsqueeze(2) \
                .broadcast_to([16, NCH - 1, CS])
            p3 = cumB[:, CS:T].rearrange("p (n i) -> p n i", i=CS)
            cc3 = cumA[:, CS:T].rearrange("p (n i) -> p n i", i=CS)
            VI(v, v.tensor_tensor(p3, cc3, pe, ALU.mult))
            VI(v, v.tensor_copy(pib[:], cumB[:]))

        blk(("vector", a6_v))

        def a6_sp(s):
            sv.wait(s)
            sd.inc(s.dma_start(out=bass.AP(cbsp, 0, [[T, 16], [1, T]]), in_=acmp[:]))
            sd.inc(s.dma_start(out=bass.AP(cbsp, 16 * T, [[T, 16], [1, T]]),
                               in_=betav[:]))
            sd.inc(s.dma_start(out=bass.AP(cbsp, 32 * T, [[T, 16], [1, T]]),
                               in_=aindv[:]))

        blk(("sync", a6_sp))
        tap("pit", lambda: cumB[:], [16, T])
        tap("aind", lambda: aindv[:], [16, T])

        for k in range(KC):
            h0 = 2 * k

            def a7_s(v, k=k):
                ss.wait(v)
                sd.wait(v)
                VI(v, v.tensor_scalar(scr[:], stripB[k][:, 0:T],
                                       lngA[:, k:k + 1], lnbA[:, k:k + 1],
                                       ALU.mult, ALU.add))

            blk(("vector", a7_s))

            for hf in range(2):
                o = hf * 1024

                def a7_d1(s, k=k, h0=h0, hf=hf, o=o):
                    sv.wait(s)
                    s.wait_ge(sd.sem, sd.n)
                    sd.inc(s.dma_start(out=BC1[0:64, :],
                                       in_=bass.AP(cbsp, (16 + h0) * T + o,
                                                   [[0, 64], [1, 1024]])))
                    sd.inc(s.dma_start(out=BC1[64:128, :],
                                       in_=bass.AP(cbsp, (17 + h0) * T + o,
                                                   [[0, 64], [1, 1024]])))
                    sd.inc(s.dma_start(out=BC2[0:64, :],
                                       in_=bass.AP(cbsp, (32 + h0) * T + o,
                                                   [[0, 64], [1, 1024]])))
                    sd.inc(s.dma_start(out=BC2[64:128, :],
                                       in_=bass.AP(cbsp, (33 + h0) * T + o,
                                                   [[0, 64], [1, 1024]])))

                def a7_v1(v, k=k, hf=hf, o=o):
                    ss.wait(v)
                    sd.wait(v)
                    VI(v, v.tensor_tensor(BFz, scr[:, o:o + 1024], BC1[:], ALU.mult))
                    VI(v, v.tensor_tensor(Qs, BFz, BC2[:], ALU.mult))
                    q3 = Qs.rearrange("p (n i) -> p n i", i=CS)
                    b3 = BFz.rearrange("p (n i) -> p n i", i=CS)
                    nh = 1024 // CS
                    if hf == 1:
                        pass
                    VI(v, v.tensor_tensor(stt[:, 0:64], b3[:, :, CS - 1],
                                           q3[:, :, CS - 1], ALU.subtract))
                    VI(v, v.tensor_tensor(BFz, BFz, Qs, ALU.subtract))

                def a7_d2(s, k=k, h0=h0, o=o):
                    sv.wait(s)
                    sd.inc(s.dma_start(out=BC2[0:64, :],
                                       in_=bass.AP(cbsp, h0 * T + o,
                                                   [[0, 64], [1, 1024]])))
                    sd.inc(s.dma_start(out=BC2[64:128, :],
                                       in_=bass.AP(cbsp, (h0 + 1) * T + o,
                                                   [[0, 64], [1, 1024]])))

                def a7_v2(v, k=k, hf=hf, o=o):
                    sd.wait(v)
                    nh = 1024 // CS
                    q3 = Qs.rearrange("p (n i) -> p n i", i=CS)
                    a3 = BC2.rearrange("p (n i) -> p n i", i=CS)
                    VI(v, v.tensor_tensor(Vs[:, 0:nh - 1], stt[:, 0:nh - 1],
                                           a3[:, 1:nh, 0], ALU.mult))
                    VI(v, v.tensor_tensor(q3[:, 1:nh, 0], q3[:, 1:nh, 0],
                                           Vs[:, 0:nh - 1], ALU.add))
                    if hf == 1:
                        VI(v, v.tensor_tensor(Vs[:, 100:101], stt[:, 64:65],
                                               a3[:, 0:1, 0], ALU.mult))
                        VI(v, v.tensor_tensor(q3[:, 0:1, 0], q3[:, 0:1, 0],
                                               Vs[:, 100:101], ALU.add))
                    VI(v, v.tensor_copy(stt[:, 64:65], stt[:, 63:64]))
                    if hf == 0:
                        VI(v, v.tensor_tensor_scan(Vs, BC2[:], Qs, 0.0,
                                                    ALU.mult, ALU.add))
                    else:
                        VI(v, v.tensor_tensor_scan(Vs, BC2[:], Qs, stt[:, 65:66],
                                                    ALU.mult, ALU.add))
                    VI(v, v.tensor_copy(stt[:, 65:66], Vs[:, 1023:1024]))
                    VI(v, v.tensor_tensor(stripB[k][:, o:o + 1024], BFz, Vs,
                                           ALU.add))
                    if hf == 1:
                        VI(v, v.tensor_tensor(cp2[:, CP_PUB:CP_PUB + 1],
                                               BFz[:, 1023:1024], Vs[:, 1023:1024],
                                               ALU.add))

                blk(("sync", a7_d1))
                blk(("vector", a7_v1))
                blk(("sync", a7_d2))
                blk(("vector", a7_v2))

            def a7_st(s, k=k):
                sv.wait(s)
                sd.inc(s.dma_start(out=bass.AP(ag1_in, 16 + 128 * k, [[1, 1], [1, 128]]),
                                   in_=cp2[:, CP_PUB:CP_PUB + 1]))

            blk(("sync", a7_st))

        tap("h0T", lambda: stripB[0][:, 0:T], [128, T], BF16)

        def a7_pubA(s):
            sv.wait(s)
            sd.inc(s.dma_start(out=bass.AP(ag1_in, 0, [[1, 1], [1, 16]]),
                               in_=cp2[0:16, CP_CHB + NCH - 1:CP_CHB + NCH]))

        blk(("sync", a7_pubA))

        flushb()

        def ag1_g(g):
            sd.wait(g)
            scc.inc(g.collective_compute(
                "AllGather", ALU.bypass, replica_groups=[list(range(8))],
                ins=[ag1_in.ap()], outs=[ag1_out.ap()]))

        blk(("gpsimd", ag1_g))

        def a8_wg(g):
            stn.wait(g)
            sg.inc(g.dma_start(out=wv(0, 8192).rearrange("p (k c) -> p k c", c=D),
                               in_=bass.AP(w_outT, 0, [[D, 128], [128 * D, KC], [1, D]])))

        blk(("gpsimd", a8_wg))

        def a8_d(s):
            scc.wait(s)
            sd.inc(s.dma_start(out=cp2[0:16, CP_AALL:CP_AALL + 8],
                               in_=bass.AP(ag1_out, 0, [[1, 16], [AG1N, 8]]), allow_slow_non_contiguous=True))
            sd.inc(s.dma_start(out=cp2[0:16, CP_BALL:CP_BALL + 512]
                               .rearrange("p (s d) -> p s d", d=HD),
                               in_=bass.AP(ag1_out, 16, [[64, 16], [AG1N, 8], [1, 64]])))

        def a8_v(v):
            sd.wait(v)
            A_ = cp2[0:16, CP_AALL:CP_AALL + 8]
            VI(v, v.tensor_tensor(A_, A_, cp2[0:16, CP_SELA:CP_SELA + 8], ALU.mult))
            VI(v, v.tensor_tensor(A_, A_, cp2[0:16, CP_SELAI:CP_SELAI + 8], ALU.add))
            B_ = cp2[0:16, CP_BALL:CP_BALL + 512]
            VI(v, v.tensor_tensor(B_, B_, cp2[0:16, CP_SELB:CP_SELB + 512], ALU.mult))
            R_ = cp2[0:16, CP_R:CP_R + HD]
            VI(v, v.memset(R_, 0.0))
            for sl in range(8):
                VI(v, v.scalar_tensor_tensor(
                    R_, R_, cp2[0:16, CP_AALL + sl:CP_AALL + sl + 1],
                    cp2[0:16, CP_BALL + sl * HD:CP_BALL + (sl + 1) * HD],
                    ALU.mult, ALU.add))

        blk(("sync", a8_d), ("vector", a8_v))
        tap("carry", lambda: cp2[0:16, CP_R:CP_R + HD], [16, HD])

        def a8_d2(s):
            sv.wait(s)
            for k in range(KC):
                sd.inc(s.dma_start(out=cp2[:, CP_CST + k:CP_CST + k + 1],
                                   in_=cp2[2 * k:2 * k + 2, CP_R:CP_R + HD]))

        blk(("sync", a8_d2))

        for nb in range(2):
            for k in range(KC):
                def a8_cm(v, k=k, nb=nb):
                    if nb == 0 and k == 0:
                        sd.wait(v)
                    ss.wait(v)
                    VI(v, v.tensor_scalar_mul(cmat[:], sest[:, k * H:(k + 1) * H],
                                               cp2[:, CP_CST + k:CP_CST + k + 1]))

                def a8_mm(te, k=k, nb=nb):
                    sv.wait(te)
                    sg.wait(te)
                    ins = te.matmul(PS1[0:16, 0:512], cmat[:],
                                    wvin(k)[:, nb * 512:(nb + 1) * 512],
                                    start=(k == 0), stop=(k == KC - 1))
                    stn.inc(ins)

                blk(("vector", a8_cm), ("tensor", a8_mm))

            def a8_ev(v, nb=nb):
                stn.wait(v)
                ss.inc(v.tensor_copy(svc[:, nb * 512:(nb + 1) * 512],
                                     PS1[0:16, 0:512]))
                v.wait_ge(ss.sem, ss.n)

            blk(("vector", a8_ev))

        tap("vc", lambda: svc[:], [16, D], BF16)

        for i in range(NT):
            c0 = i * 128

            def a9_t(te, i=i, c0=c0):
                ss.wait(te)
                sg.wait(te)
                sv.wait(te)
                for nb in range(2):
                    for k in range(KC):
                        te.matmul(PQ[:, nb * 512:(nb + 1) * 512],
                                  stripB[k][:, i * 128:(i + 1) * 128],
                                  wvin(k)[:, nb * 512:(nb + 1) * 512],
                                  start=(k == 0), stop=False)
                    ins = te.matmul(PQ[:, nb * 512:(nb + 1) * 512],
                                    pib[:, c0:c0 + 128],
                                    svc[:, nb * 512:(nb + 1) * 512],
                                    start=False, stop=True)
                stn.inc(ins)

            def a9_v(v, i=i):
                stn.wait(v)
                VI(v, v.reduce_sum(stt[:, 0:1], PQ[:, 0:1024], AX.X))
                VI(v, v.tensor_copy(scr[:, 0:1024], PQ[:, 0:1024]))
                VI(v, v.tensor_tensor(scr[:, 0:1024], scr[:, 0:1024],
                                       scr[:, 0:1024], ALU.mult))
                VI(v, v.reduce_sum(stt[:, 1:2], scr[:, 0:1024], AX.X))

            def a9_v2(v, i=i):
                ss.wait(v)
                VI(v, v.tensor_scalar_mul(stt[:, 2:3], stt[:, 0:1], 1.0 / D))
                VI(v, v.tensor_tensor(stt[:, 4:5], stt[:, 2:3], stt[:, 2:3], ALU.mult))
                VI(v, v.tensor_scalar_mul(stt[:, 3:4], stt[:, 1:2], 1.0 / D))
                VI(v, v.tensor_tensor(stt[:, 3:4], stt[:, 3:4], stt[:, 4:5],
                                       ALU.subtract))

            def a9_s2(sc, i=i):
                sv.wait(sc)
                SI(sc, sc.activation(stt[:, 3:4], stt[:, 3:4], AF.Sqrt, bias=seps[:]))

            def a9_v3(v, i=i):
                ss.wait(v)
                VI(v, v.reciprocal(stt[:, 3:4], stt[:, 3:4]))
                VI(v, v.tensor_scalar(scr[:, 0:1024], PQ[:, 0:1024], stt[:, 2:3],
                                       stt[:, 3:4], ALU.subtract, ALU.mult))
                VI(v, v.tensor_tensor(cur[:, i * D:(i + 1) * D],
                                       cur[:, i * D:(i + 1) * D], scr[:, 0:1024],
                                       ALU.add))

            def a9_sp(s, i=i):
                sv.wait(s)
                sd.inc(s.dma_start(out=bass.AP(x1sp, i * 128 * D, [[D, 128], [1, D]]),
                                   in_=cur[:, i * D:(i + 1) * D]))

            blk(("tensor", a9_t), ("vector", a9_v))
            blk(("vector", a9_v2))
            blk(("scalar", a9_s2))
            blk(("vector", a9_v3), ("sync", a9_sp))

        tap("x1", lambda: cur[:, 0:D], [128, D])

        emit_ln_stats()

        def a10_v(v):
            ss.wait(v)
            sd.wait(v)
            for i in range(NT):
                VI(v, v.tensor_scalar(cur[:, i * D:(i + 1) * D],
                                       cur[:, i * D:(i + 1) * D],
                                       stt[:, 32 + i:33 + i], stt[:, 48 + i:49 + i],
                                       ALU.subtract, ALU.mult))

        blk(("vector", a10_v))
        tap("lnpre", lambda: cur[:, 0:D], [128, D])
        tap("lnpre_all", lambda: cur[:], [128, NT * D])

        def qsl(i):
            return (qtt5[:, (i) * 512:(i + 1) * 512] if i < 5 else qtt)

        def emit_attn(i, L):
            key0 = i * 128

            def s_mk(s, i=i):
                ss.wait(s)
                sv.wait(s)
                sd.inc(s.dma_start(out=mkt[:],
                                   in_=bass.AP(maskt, i * 128 * SPAN,
                                               [[SPAN, 128], [1, SPAN]])))

            blk(("sync", s_mk))

            for j in range(4):
                def t_sim(te, i=i, j=j):
                    ss.wait(te)
                    sd.wait(te)
                    sg.wait(te)
                    sv.wait(te)
                    for hh in range(2):
                        po = hh * 64
                        pd = [PS0, PS1][hh]
                        qs_ = qsl(i)
                        te.matmul(pd[:, 0:512], qs_[po:po + 64, j * 128:(j + 1) * 128],
                                  kts[j][po:po + 64, key0:key0 + 512],
                                  start=True, stop=True)
                        ins = te.matmul(pd[:, 512:SPAN],
                                        qs_[po:po + 64, j * 128:(j + 1) * 128],
                                        kts[j][po:po + 64, key0 + 512:key0 + SPAN],
                                        start=True, stop=True)
                    stn.inc(ins)

                def v_mask(v, i=i, j=j):
                    stn.wait(v)
                    sd.wait(v)
                    ss.wait(v)
                    VI(v, v.tensor_tensor(smask[:, 0:SPAN], PS0[:], mkt[:], ALU.add))
                    VI(v, v.tensor_tensor(smask[:, SPAN:2 * SPAN], PS1[:], mkt[:],
                                           ALU.add))

                def g_max(v, i=i, j=j):
                    m3 = smask[:].rearrange("p (h s) -> p h s", s=SPAN)
                    VI(v, v.tensor_reduce(stt[:, 32:34], m3, AX.X, ALU.max,
                                           negate=True))

                blk(("tensor", t_sim), ("vector", v_mask))
                blk(("vector", g_max))

                def s_exp2(sc, i=i, j=j):
                    sv.wait(sc)
                    stn.wait(sc)
                    SI(sc, sc.activation(
                        pbuf[:, 0:SPAN], smask[:, 0:SPAN], AF.Exp,
                        bias=stt[:, 32:33], accum_out=stt[:, 34:35]))
                    SI(sc, sc.activation(
                        pbuf2[:, 0:SPAN], smask[:, SPAN:2 * SPAN], AF.Exp,
                        bias=stt[:, 33:34], accum_out=stt[:, 35:36]))

                def v_rc2(v, i=i, j=j):
                    ss.wait(v)
                    VI(v, v.reciprocal(stt[:, 36:38], stt[:, 34:36]))
                    VI(v, v.tensor_scalar_mul(dbb[:], sid[:], stt[:, 36:37]))
                    VI(v, v.tensor_scalar_mul(dbb2[:], sid[:], stt[:, 37:38]))

                blk(("scalar", s_exp2))
                blk(("vector", v_rc2))

                for hh in range(2):
                    h = 2 * j + hh
                    pb = pbuf if hh == 0 else pbuf2
                    db = dbb if hh == 0 else dbb2

                    def t_pt(te, i=i, hh=hh, pb=pb, db=db):
                        sv.wait(te)
                        ss.wait(te)
                        for kc in range(5):
                            ins = te.matmul(
                                PS0[:, kc * 128:(kc + 1) * 128],
                                pb[:, kc * 128:(kc + 1) * 128],
                                db[:], start=True, stop=True)
                        stn.inc(ins)

                    def s_pt(v, hh=hh):
                        stn.wait(v)
                        ss.inc(v.tensor_copy(ptb[:], PS0[:, 0:SPAN]))
                        v.wait_ge(ss.sem, ss.n)

                    def t_pv(te, i=i, j=j, hh=hh, h=h):
                        ss.wait(te)
                        for kc in range(5):
                            ins = te.matmul(
                                PQ[hh * 64:(hh + 1) * 64, 1024:1152],
                                vts[:, (i + kc) * INNER + h * DH:
                                    (i + kc) * INNER + (h + 1) * DH],
                                ptb[:, kc * 128:(kc + 1) * 128],
                                start=(kc == 0), stop=(kc == 4))
                        stn.inc(ins)

                    blk(("tensor", t_pt), ("vector", s_pt))
                    blk(("tensor", t_pv))

                def s_av(v, i=i, j=j):
                    stn.wait(v)
                    ss.inc(v.tensor_copy(stripA[j][:, i * 128:(i + 1) * 128],
                                         PQ[:, 1024:1152]))
                    v.wait_ge(ss.sem, ss.n)

                blk(("vector", s_av))

        def emit_qkv(i, L, do_attn):
            def t_mm(te, i=i):
                ss.wait(te)
                sg.wait(te)
                sv.wait(te)
                for nb in range(3):
                    for k in range(KC):
                        off = 1 if k < 4 else 0
                        ins = te.matmul(
                            PQ[:, nb * 512:(nb + 1) * 512],
                            stripA[k][:, off + i * 128: off + (i + 1) * 128],
                            wvq(k)[:, nb * 512:(nb + 1) * 512],
                            start=(k == 0), stop=(k == KC - 1))
                stn.inc(ins)

            def s_rot(s, i=i):
                sv.wait(s)
                sd.inc(s.dma_start(out=rtt[:],
                                   in_=bass.AP(rotpk, i * 128 * 256,
                                               [[256, 128], [1, 256]])))

            def s_sqn(v, i=i):
                stn.wait(v)
                VI(v, v.tensor_copy(scr[:, 0:1024], PQ[:, 0:1024]))
                ss.inc(v.tensor_tensor(scr[:, 0:1024], scr[:, 0:1024],
                                       scr[:, 0:1024], ALU.mult))
                v.wait_ge(ss.sem, ss.n)


            def v_n(v, i=i):
                stn.wait(v)
                sd.wait(v)
                ss.wait(v)
                VI(v, v.tensor_tensor(rtt[:], rtt[:], qkp[:], ALU.mult))
                q3 = scr[:, 0:512].rearrange("p (h d) -> p h d", d=DH)
                k3 = scr[:, 512:1024].rearrange("p (h d) -> p h d", d=DH)
                VI(v, v.tensor_reduce(stt[:, 0:8], q3, AX.X, ALU.add))
                VI(v, v.tensor_reduce(stt[:, 8:16], k3, AX.X, ALU.add))

            def s_n(sc, i=i):
                sv.wait(sc)
                SI(sc, sc.activation(stt[:, 16:32], stt[:, 0:16], AF.Sqrt))

            def s_n_cp(v, i=i):
                stn.wait(v)
                ss.inc(v.tensor_copy(vts[:, (i + 4) * INNER:(i + 5) * INNER],
                                     PQ[:, 1024:1536]))
                v.wait_ge(ss.sem, ss.n)

            def v_rot(v, i=i):
                ss.wait(v)
                VI(v, v.tensor_scalar_max(stt[:, 16:32], stt[:, 16:32], 1e-12))
                VI(v, v.reciprocal(stt[:, 16:32], stt[:, 16:32]))
                for qk in range(2):
                    rb = stt[:, 16 + 8 * qk:24 + 8 * qk].unsqueeze(2) \
                        .broadcast_to([128, 8, 64])
                    p3 = PQ[:, qk * 512:(qk + 1) * 512] \
                        .rearrange("p (h d) -> p h d", d=DH)
                    t13 = scr[:, 0:512].rearrange("p (h d) -> p h d", d=DH)
                    VI(v, v.tensor_tensor(t13, p3, rb, ALU.mult))
                    cs_t = rtt[:, 128 * qk:128 * qk + 64].unsqueeze(1) \
                        .broadcast_to([128, 8, 64])
                    sn_t = rtt[:, 128 * qk + 64:128 * qk + 128].unsqueeze(1) \
                        .broadcast_to([128, 8, 64])
                    u3 = scr[:, 512:1024].rearrange("p (h d) -> p h d", d=DH)
                    VI(v, v.tensor_tensor(u3, t13, cs_t, ALU.mult))
                    w3 = scr[:, 1024:1536].rearrange("p (h d) -> p h d", d=DH)
                    VI(v, v.tensor_tensor(w3, rotread(scr[:, 0:512], 8, 64),
                                           sn_t, ALU.mult))
                    dst = tlb2[:, qk * 512:(qk + 1) * 512]
                    VI(v, v.tensor_tensor(dst, scr[:, 512:1024], scr[:, 1024:1536],
                                           ALU.add))

            def t_tr(te, i=i):
                sv.wait(te)
                ss.wait(te)
                for jj in range(4):
                    te.transpose(PTR[:, jj * 128:(jj + 1) * 128],
                                 tlb2[:, jj * 128:(jj + 1) * 128], sid[:])
                for jj in range(4):
                    ins = te.transpose(PTR[:, 512 + jj * 128:512 + (jj + 1) * 128],
                                       tlb2[:, 512 + jj * 128:512 + (jj + 1) * 128],
                                       sid[:])
                stn.inc(ins)

            def s_tr(v, i=i):
                stn.wait(v)
                ins = None
                for jj in range(4):
                    ins = v.tensor_copy(qsl(i)[:, jj * 128:(jj + 1) * 128],
                                        PTR[:, jj * 128:(jj + 1) * 128])
                for jj in range(4):
                    ins = v.tensor_copy(kts[jj][:, WIN + i * 128: WIN + (i + 1) * 128],
                                        PTR[:, 512 + jj * 128:512 + (jj + 1) * 128])
                ss.inc(ins)

            blk(("tensor", t_mm), ("sync", s_rot))
            blk(("vector", s_sqn))
            if L == 0 and i == 5:
                tap("sq5", lambda: scr[:, 0:1024], [128, 1024])
            blk(("vector", v_n))
            blk(("scalar", s_n), ("vector", s_n_cp))
            if L == 0 and i == 5:
                tap("rtt5", lambda: rtt[:], [128, 256])
                tap("rn5", lambda: stt[:, 0:48], [128, 48])
            blk(("vector", v_rot))
            if L == 0 and i == 5:
                tap("t13", lambda: scr[:, 0:1536], [128, 1536])
                tap("qrot5", lambda: tlb2[:], [128, 1024], BF16)
            blk(("tensor", t_tr), ("vector", s_tr))
            if L == 0 and i == 5:
                tap("q5", lambda: qtt[:], [128, 512], BF16)
                tap("k5", lambda: kts[0][:, WIN + 5 * 128:WIN + 6 * 128], [128, 128], BF16)
                tap("v5", lambda: vts[:, 9 * INNER:10 * INNER], [128, 512], BF16)
            if do_attn:
                emit_attn(i, L)
                if L == 0 and i == 5:
                    tap("p5", lambda: pbuf[:, SPAN:2 * SPAN], [128, SPAN], BF16)
                    tap("o5", lambda: stripA[0][:, 5 * 128:6 * 128], [128, 128], BF16)

        for L in range(2):
            def b0_g(g, L=L):
                stn.wait(g)
                sg.inc(g.dma_start(
                    out=wv(0, 12288).rearrange("p (k c) -> p k c", c=1536),
                    in_=bass.AP(wqkv, L * D * 3 * INNER,
                                [[3 * INNER, 128], [128 * 3 * INNER, KC],
                                 [1, 3 * INNER]])))

            def b0_s(s, L=L):
                sv.wait(s)
                ss.wait(s)
                sd.inc(s.dma_start(out=lngF[:],
                                   in_=bass.AP(f_ln_g, L * D, [[1, 128], [128, KC]]), allow_slow_non_contiguous=True))
                sd.inc(s.dma_start(out=lnbF[:],
                                   in_=bass.AP(f_ln_b, L * D, [[1, 128], [128, KC]]), allow_slow_non_contiguous=True))
                sd.inc(s.dma_start(out=qkp[:],
                                   in_=bass.AP(qkpat, L * 128 * 256,
                                               [[256, 128], [1, 256]])))

            blk(("gpsimd", b0_g), ("sync", b0_s))

            for i in range(NT):
                def b1_v(v, i=i):
                    ss.wait(v)
                    VI(v, v.tensor_copy(tlb[:], cur[:, i * D:(i + 1) * D]))
                blk(("vector", b1_v))
                emit_tile_transpose(stripA, 1, i)

            for i in range(1, NT):
                emit_qkv(i, L, do_attn=(i >= 5))

            def b3_pack(s, L=L):
                ss.wait(s)
                off = 0
                for j in range(4):
                    sd.inc(s.dma_start(
                        out=bass.AP(agk_in[L], off, [[512, 128], [1, 512]]),
                        in_=kts[j][:, WIN + T - 512: WIN + T]))
                    off += 128 * 512
                for jt in range(4):
                    sd.inc(s.dma_start(
                        out=bass.AP(agk_in[L], off, [[512, 128], [1, 512]]),
                        in_=vts[:, (NT + jt) * INNER:(NT + 1 + jt) * INNER]))
                    off += 128 * 512
                for k in range(KC):
                    sd.inc(s.dma_start(
                        out=bass.AP(agk_in[L], off + k * 128, [[1, 1], [1, 128]]),
                        in_=stripA[k][:, T:T + 1]))

            def b3_cc(g, L=L):
                sd.wait(g)
                scc.inc(g.collective_compute(
                    "AllGather", ALU.bypass, replica_groups=[list(range(8))],
                    ins=[agk_in[L].ap()], outs=[agk_out[L].ap()]))

            def b3_recv(g, L=L):
                scc.wait(g)
                for jj in range(9):
                    with g.register() as gj:
                        g.reg_load(gj, soffk[0:1, 0:1])
                        g.reg_add(gj, gj, jj * 128 * 512)
                        if jj < 4:
                            src = bass.AP(agk_out[L], gj, [[512, 128], [1, 512]])
                            ins = g.dma_start(out=kts[jj][:, 0:WIN], in_=src)
                        elif jj < 8:
                            src = bass.AP(agk_out[L], gj, [[512, 128], [1, 512]])
                            ins = g.dma_start(
                                out=vts[:, (jj - 4) * INNER:(jj - 3) * INNER], in_=src)
                        else:
                            src = bass.AP(agk_out[L], gj, [[1, 128], [128, KC]])
                            ins = g.dma_start(out=pcol[:], in_=src,
                                              allow_slow_non_contiguous=True)
                        sg.inc(ins)

            blk(("sync", b3_pack))
            flushb()
            blk(("gpsimd", b3_cc))
            flushb()
            blk(("gpsimd", b3_recv))

            def b3_lead(v):
                sg.wait(v)
                for k in range(KC):
                    VI(v, v.tensor_scalar_mul(stripA[k][:, 0:1], pcol[:, k:k + 1],
                                               sqz[:]))

            blk(("vector", b3_lead))

            emit_qkv(0, L, do_attn=True)
            if L == 0:
                tap("q0t", lambda: qtt5[:, 0:512], [128, 512], BF16)
                tap("k0t", lambda: kts[0][:, 512:640], [128, 128], BF16)
                tap("o0t", lambda: stripA[0][:, 0:128], [128, 128], BF16)
                tap("p0", lambda: pbuf[:, 0:SPAN], [128, SPAN], BF16)
                tap("kh0", lambda: kts[0][:, 0:512], [128, 512], BF16)
                tap("vh0", lambda: vts[:, 0:512], [128, 512], BF16)
            for i in range(1, 5):
                emit_attn(i, L)

            def b5_g(g, L=L):
                stn.wait(g)
                sg.inc(g.dma_start(
                    out=wv(0, 4096).rearrange("p (k c) -> p k c", c=D),
                    in_=bass.AP(wo, L * INNER * D, [[D, 128], [128 * D, 4], [1, D]])))

            blk(("gpsimd", b5_g))

            for i in range(NT - 1, -1, -1):
                def b5_t(te, i=i):
                    ss.wait(te)
                    sg.wait(te)
                    sv.wait(te)
                    for nb in range(2):
                        for j in range(4):
                            ins = te.matmul(PQ[:, nb * 512:(nb + 1) * 512],
                                            stripA[j][:, i * 128:(i + 1) * 128],
                                            wvo(j)[:, nb * 512:(nb + 1) * 512],
                                            start=(j == 0), stop=(j == 3))
                    stn.inc(ins)

                def b5_v(v, i=i):
                    stn.wait(v)
                    VI(v, v.tensor_tensor(cur[:, i * D:(i + 1) * D],
                                           cur[:, i * D:(i + 1) * D],
                                           PQ[:, 0:1024], ALU.add))
                    VI(v, v.tensor_copy(tlb[:], cur[:, i * D:(i + 1) * D]))

                blk(("tensor", b5_t), ("vector", b5_v))
                emit_tile_transpose(stripA, 1, i)

            tap(f"x2T_{L}", lambda: stripA[0][:, 1:1 + T], [128, T], BF16)
            tap(f"x2cur_{L}", lambda: cur[:], [128, NT * D])

            def b6_pack(s, L=L):
                ss.wait(s)
                for k in range(KC):
                    sd.inc(s.dma_start(
                        out=bass.AP(agc_in[L], k * 128, [[1, 1], [1, 128]]),
                        in_=stripA[k][:, T:T + 1]))

            def b6_cc(g, L=L):
                sd.wait(g)
                scc.inc(g.collective_compute(
                    "AllGather", ALU.bypass, replica_groups=[list(range(8))],
                    ins=[agc_in[L].ap()], outs=[agc_out[L].ap()]))

            def b6_recv(g, L=L):
                scc.wait(g)
                with g.register() as gc:
                    g.reg_load(gc, soffc[0:1, 0:1])
                    src = bass.AP(agc_out[L], gc, [[1, 128], [128, KC]])
                    sg.inc(g.dma_start(out=pcol[:], in_=src,
                                       allow_slow_non_contiguous=True))

            blk(("sync", b6_pack))
            flushb()
            blk(("gpsimd", b6_cc))
            flushb()
            blk(("gpsimd", b6_recv))

            def b6_lead(v):
                sg.wait(v)
                for k in range(KC):
                    VI(v, v.tensor_scalar_mul(stripA[k][:, 0:1], pcol[:, k:k + 1],
                                               sqz[:]))

            blk(("vector", b6_lead))

            def shifted(k, ap2, c0, n):
                off = 1 if k < 4 else 0
                return ap2[:, off + c0: off + c0 + n]

            STR = [PQ[0:1, 0:512], PQ[0:1, 512:1024], PQ[0:1, 1024:1536],
                   PS0[0:1, 0:512]]
            def b7_st1(te):
                ss.wait(te)
                sv.wait(te)
                for k in range(KC):
                    for cb in range(4):
                        ins = te.matmul(STR[cb], so_c[:],
                                        shifted(k, stripA[k], cb * 512, 512),
                                        start=(k == 0), stop=(k == KC - 1))
                stn.inc(ins)

            def b7_ev1(v):
                stn.wait(v)
                for cb in range(4):
                    VI(v, v.tensor_copy(scr[0:1, cb * 512:(cb + 1) * 512], STR[cb]))

            blk(("tensor", b7_st1))
            blk(("vector", b7_ev1))

            for k in range(KC):
                def b7_sq(v, k=k):
                    stn.wait(v)
                    ss.inc(v.tensor_tensor(sqs[:, 0:1 + T], stripA[k][:, 0:1 + T],
                                           stripA[k][:, 0:1 + T], ALU.mult))
                    v.wait_ge(ss.sem, ss.n)

                def b7_st2(te, k=k):
                    ss.wait(te)
                    if k == 0:
                        sv.wait(te)
                    for cb in range(4):
                        ins = te.matmul(STR[cb], so_c[:],
                                        shifted(k, sqs, cb * 512, 512),
                                        start=(k == 0), stop=(k == KC - 1))
                    stn.inc(ins)

                blk(("vector", b7_sq), ("tensor", b7_st2))

            def b7_ev2(v):
                stn.wait(v)
                spots = [cp2[0:1, 0:512], cp2[0:1, 512:1024],
                         cp2[0:1, 1024:1536], arena[0:1, 19337:19849]]
                for cb in range(4):
                    VI(v, v.tensor_copy(spots[cb], STR[cb]))

            blk(("vector", b7_ev2))

            spots = [cp2[0:1, 0:512], cp2[0:1, 512:1024],
                     cp2[0:1, 1024:1536], arena[0:1, 19337:19849]]
            t_mu = arena[0:1, 19849:20361]
            t_m2 = arena[0:1, 20361:20873]
            t_ms = arena[0:1, 20873:21385]
            for cb in range(4):
                def b7_rm1(v, cb=cb):
                    if cb == 0:
                        stn.wait(v)
                    ss.wait(v)
                    sd.wait(v)
                    s1b = scr[0:1, cb * 512:(cb + 1) * 512]
                    s2b = spots[cb]
                    VI(v, v.tensor_scalar_mul(t_mu, s1b, 1.0 / D))
                    VI(v, v.tensor_scalar_mul(s2b, s2b, 1.0 / D))
                    VI(v, v.tensor_tensor(t_m2, t_mu, t_mu, ALU.mult))
                    VI(v, v.tensor_tensor(s2b, s2b, t_m2, ALU.subtract))

                def b7_rm2(sc, cb=cb):
                    sv.wait(sc)
                    SI(sc, sc.activation(spots[cb], spots[cb], AF.Sqrt,
                                         bias=seps[0:1, :]))

                def b7_rm3(v, cb=cb):
                    ss.wait(v)
                    VI(v, v.reciprocal(spots[cb], spots[cb]))
                    VI(v, v.tensor_tensor(t_ms, t_mu, spots[cb], ALU.mult))

                def b7_rm4(s, cb=cb):
                    sv.wait(s)
                    sd.inc(s.dma_start(out=bass.AP(rwsp, cb * 512, [[512, 1], [1, 512]]),
                                       in_=spots[cb]))
                    sd.inc(s.dma_start(out=bass.AP(rwsp, T + cb * 512,
                                                   [[512, 1], [1, 512]]),
                                       in_=t_ms))

                blk(("vector", b7_rm1))
                blk(("scalar", b7_rm2))
                blk(("vector", b7_rm3), ("sync", b7_rm4))

            for hf in (1, 0):
                def b7_bc(s, hf=hf):
                    sv.wait(s)
                    ss.wait(s)
                    s.wait_ge(sd.sem, sd.n)
                    sd.inc(s.dma_start(out=bcr1[:],
                                       in_=bass.AP(rwsp, hf * 1024,
                                                   [[0, 128], [1, 1024]])))
                    sd.inc(s.dma_start(out=bcr2[:],
                                       in_=bass.AP(rwsp, T + hf * 1024,
                                                   [[0, 128], [1, 1024]])))

                blk(("sync", b7_bc))
                for k in range(KC):
                    def b7_apv(v, k=k, hf=hf):
                        sd.wait(v)
                        ss.wait(v)
                        VI(v, v.tensor_tensor(bscr,
                                               shifted(k, stripA[k], hf * 1024, 1024),
                                               bcr1[:], ALU.mult))
                        VI(v, v.tensor_tensor(bscr, bscr, bcr2[:], ALU.subtract))

                    def b7_aps(v, k=k, hf=hf):
                        VI(v, v.tensor_scalar(
                            stripA[k][:, 1 + hf * 1024: 1 + (hf + 1) * 1024],
                            bscr, lngF[:, k:k + 1], lnbF[:, k:k + 1],
                            ALU.mult, ALU.add))

                    blk(("vector", b7_apv))
                    blk(("vector", b7_aps))

            tap(f"ln2T_{L}", lambda: stripA[0][:, 1:1 + T], [128, T], BF16)

            def b7_bg(g, L=L):
                sg.inc(g.dma_start(out=sb2r[:],
                                   in_=bass.AP(b2, L * D, [[1, 1], [1, D]])))

            blk(("gpsimd", b7_bg))

            def emit_wg(g_i, L=L):
                gw, g0 = W1GW[g_i], W1G0[g_i]
                bofs = 0
                sofs = 0

                def b7_wg(g, L=L, gw=gw, g0=g0, bofs=bofs, sofs=sofs):
                    stn.wait(g)
                    for half in range(2):
                        o3 = wv(bofs + half * KC * WSL,
                                bofs + (half + 1) * KC * WSL) \
                            .rearrange("p (k c) -> p k c", c=WSL)[:, :, 0:gw]
                        sg.inc(g.dma_start(
                            out=o3,
                            in_=bass.AP(w1, L * D * 2 * HID + half * HID + g0,
                                        [[2 * HID, 128], [128 * 2 * HID, KC],
                                         [1, gw]])))
                        sg.inc(g.dma_start(
                            out=sb1g[:, sofs + half * WSL:sofs + half * WSL + gw],
                            in_=bass.AP(b1, L * 2 * HID + half * HID + g0,
                                        [[1, 1], [1, gw]])))

                blk(("gpsimd", b7_wg))
                flushb()
                return sg.n

            wgn = {}
            w2n = {}
            for g_i in range(NW1G):
                gw, g0 = W1GW[g_i], W1G0[g_i]
                nchk = (gw + 127) // 128
                bofs = 0
                sofs = 0
                wgn[g_i] = emit_wg(g_i)
                if g_i == NW1G - 1:
                    def b7_w2g0(g, L=L):
                        sg.inc(g.dma_start(
                            out=w2h[:].rearrange("p (k c) -> p k c", c=512),
                            in_=bass.AP(w2p, L * HIDP * D,
                                        [[D, 128], [128 * D, KH], [1, 512]])))

                    blk(("gpsimd", b7_w2g0))
                    flushb()
                    w2n[0] = sg.n

                def wv1(k, half, bofs=bofs):
                    return wv(bofs + (half * KC + k) * WSL,
                              bofs + (half * KC + k + 1) * WSL)

                for i in range(NT):
                    def b7_mm(te, i=i, gw=gw, g_i=g_i, wv1=wv1, sofs=sofs,
                              sgn=wgn[g_i]):
                        te.wait_ge(sg.sem, sgn)
                        ss.wait(te)
                        sv.wait(te)
                        for ag in range(2):
                            pd = [PS0, PS1][ag]
                            for k in range(KC):
                                te.matmul(pd[:, 0:gw],
                                          stripA[k][:, 1 + i * 128: 1 + (i + 1) * 128],
                                          wv1(k, ag)[:, 0:gw],
                                          start=(k == 0), stop=False)
                            ins = te.matmul(pd[:, 0:gw], so_r[:],
                                            sb1g[:, sofs + ag * WSL:
                                                 sofs + ag * WSL + gw],
                                            start=False, stop=True)
                        stn.inc(ins)

                    def b7_ge(sc, i=i, gw=gw):
                        stn.wait(sc)
                        SI(sc, sc.activation(tlb[:, 0:gw], PS1[:, 0:gw], AF.Gelu))

                    def b7_mu(v, i=i, gw=gw):
                        ss.wait(v)
                        VI(v, v.tensor_tensor(tlb2[:, 0:gw], PS0[:, 0:gw],
                                               tlb[:, 0:gw], ALU.mult))

                    def b7_tr(te, i=i, gw=gw, nchk=nchk):
                        sv.wait(te)
                        sd.wait(te)
                        ss.wait(te)
                        for c in range(nchk):
                            c0c, c1c = c * 128, min((c + 1) * 128, gw)
                            ins = te.transpose(PTR[0:c1c - c0c, c * 128:c * 128 + 128],
                                               tlb2[:, c0c:c1c], sid[:])
                        stn.inc(ins)

                    def b7_ev(v, i=i, gw=gw, nchk=nchk):
                        stn.wait(v)
                        ins = None
                        for c in range(nchk):
                            c0c, c1c = c * 128, min((c + 1) * 128, gw)
                            ins = v.tensor_copy(hgt[0:c1c - c0c, c * 128:c * 128 + 128],
                                                PTR[0:c1c - c0c, c * 128:c * 128 + 128])
                        ss.inc(ins)

                    def b7_sp(s, i=i, g_i=g_i, gw=gw, nchk=nchk, g0=g0):
                        ss.wait(s)
                        ch0 = g0 // 128
                        for c in range(nchk):
                            c0c, c1c = c * 128, min((c + 1) * 128, gw)
                            sd.inc(s.dma_start(
                                out=bass.AP(hgesp, (i * KH + ch0 + c) * 128 * 128,
                                            [[128, c1c - c0c], [1, 128]]),
                                in_=hgt[0:c1c - c0c, c * 128:c * 128 + 128]))
                        if g_i == NW1G - 1:
                            sd.inc(s.dma_start(
                                out=bass.AP(hgesp, (i * KH + 21) * 128 * 128 + 42 * 128,
                                            [[128, 86], [1, 128]]),
                                in_=zzb[0:86, 0:128]))

                    blk(("tensor", b7_mm))
                    blk(("scalar", b7_ge))
                    blk(("vector", b7_mu))
                    blk(("tensor", b7_tr), ("vector", b7_ev), ("sync", b7_sp))


            def b7_w2g1(g, L=L):
                stn.wait(g)
                sg.inc(g.dma_start(
                    out=wv(0, KH * 512).rearrange("p (k c) -> p k c", c=512),
                    in_=bass.AP(w2p, L * HIDP * D + 512,
                                [[D, 128], [128 * D, KH], [1, 512]])))

            blk(("gpsimd", b7_w2g1))
            flushb()
            w2n[1] = sg.n

            for hf in range(2):
                wsrc = w2h if hf == 0 else wv(0, KH * 512)
                for i in range(NT):
                    def b7_ld(s, i=i):
                        stn.wait(s)
                        s.wait_ge(sd.sem, sd.n)
                        sd.inc(s.dma_start(
                            out=hgt[:].rearrange("p (k c) -> p k c", c=128),
                            in_=bass.AP(hgesp, i * KH * 128 * 128,
                                        [[128, 128], [128 * 128, KH], [1, 128]])))

                    def b7_w2mm(te, i=i, hf=hf, wsrc=wsrc, sgn=w2n[hf]):
                        sd.wait(te)
                        te.wait_ge(sg.sem, sgn)
                        sv.wait(te)
                        for k in range(KH):
                            te.matmul(PQ[:, 0:512], hgt[:, k * 128:(k + 1) * 128],
                                      wsrc[:, k * 512:(k + 1) * 512],
                                      start=(k == 0), stop=False)
                        ins = te.matmul(PQ[:, 0:512], so_r[:],
                                        sb2r[:, hf * 512:(hf + 1) * 512],
                                        start=False, stop=True)
                        stn.inc(ins)

                    def b7_res(v, i=i, hf=hf):
                        stn.wait(v)
                        VI(v, v.tensor_tensor(
                            cur[:, i * D + hf * 512: i * D + (hf + 1) * 512],
                            cur[:, i * D + hf * 512: i * D + (hf + 1) * 512],
                            PQ[:, 0:512], ALU.add))

                    blk(("sync", b7_ld))
                    blk(("tensor", b7_w2mm), ("vector", b7_res))

            tap(f"x3_{L}", lambda: cur[:, 0:D], [128, D])

        emit_ln_stats()
        for i in range(NT):
            def c_ld(s, i=i):
                sv.wait(s)
                ss.wait(s)
                s.wait_ge(sd.sem, sd.n)
                sd.inc(s.dma_start(out=scr[:, 1024:2048],
                                   in_=bass.AP(x1sp, i * 128 * D, [[D, 128], [1, D]])))

            def c_v2(v, i=i):
                sd.wait(v)
                VI(v, v.tensor_scalar(cur[:, i * D:(i + 1) * D],
                                       cur[:, i * D:(i + 1) * D],
                                       stt[:, 32 + i:33 + i], stt[:, 48 + i:49 + i],
                                       ALU.subtract, ALU.mult))
                VI(v, v.tensor_tensor(cur[:, i * D:(i + 1) * D],
                                       cur[:, i * D:(i + 1) * D],
                                       scr[:, 1024:2048], ALU.add))

            def c_st(s, i=i):
                sv.wait(s)
                fin.inc(s.dma_start(out=bass.AP(out, i * 128 * D, [[D, 128], [1, D]]),
                                    in_=cur[:, i * D:(i + 1) * D]))

            blk(("sync", c_ld))
            blk(("vector", c_v2), ("sync", c_st))

        def c_fin(s):
            fin.wait(s)

        blk(("sync", c_fin))
        flushb()

    import os
    if os.environ.get('KPRINT'):
        print('sem counts:', {'sd': sd.n, 'sg': sg.n, 'sv': sv.n, 'ss': ss.n,
              'stn': stn.n, 'sq': sq.n, 'scc': scc.n, 'fin': fin.n})
    return nc, taps


def _host_tables():
    half = np.arange(0, DH, 2, dtype=np.float64)
    inv_freq = 1.0 / 10000.0 ** (half / DH)
    t = np.arange(S, dtype=np.float64)
    freqs = t[:, None] * inv_freq[None, :]
    freqs = np.concatenate([freqs, freqs], axis=-1)
    base = (half + 0.4 * DH) / (1.4 * DH)
    power = (t - S // 2) / float(WIN // 2)
    scale = base[None, :] ** power[:, None]
    scale = np.concatenate([scale, scale], axis=-1)
    cos, sin = np.cos(freqs), np.sin(freqs)
    sign = np.concatenate([-np.ones(32), np.ones(32)])
    csq = cos * scale * 8.0
    snq = sin * scale * 8.0 * sign[None, :]
    csk = cos / scale
    snk = sin / scale * sign[None, :]
    return np.concatenate([csq, snq, csk, snk], axis=1).astype(np.float32)


_ROT = None


def _masks(t0):
    m = np.zeros((NT, 128, SPAN), np.float32)
    r = np.arange(128)[None, :, None]
    jj = np.arange(SPAN)[None, None, :]
    i = np.arange(NT)[:, None, None]
    absk = t0 + i * 128 - WIN + jj
    allowed = (jj >= r) & (jj <= r + WIN) & (absk >= 0)
    m[~allowed] = -3.0e38
    return m.astype(ml_dtypes.bfloat16)


def make_inputs(inputs, nc_dbg=None):
    global _ROT
    if _ROT is None:
        _ROT = _host_tables()
    x = np.asarray(inputs["L0_attn_x"])
    base = {
        "w_inT": _f(np.asarray(inputs["L0_attn_W_in"]).T),
        "w_gateT": _f(np.asarray(inputs["L0_attn_W_gate"]).T),
        "a_ln_g": _f(inputs["L0_attn_ln_g"]),
        "a_ln_b": _f(inputs["L0_attn_ln_b"]),
        "bgate": _f(inputs["L0_attn_b_gate"]),
        "eigr": _f(inputs["L0_attn_eig_raw"]),
        "w_outT": _f(np.asarray(inputs["L0_attn_W_out"]).T),
        "wqkv": _f(inputs["L0_ffn_wqkv"]),
        "wo": _f(inputs["L0_ffn_wo"]),
        "w1": _f(inputs["L0_ffn_w1"]),
        "b1": _f(inputs["L0_ffn_b1"]),
        "w2p": _f(np.pad(np.asarray(inputs["L0_ffn_w2"], np.float32),
                         ((0, 0), (0, HIDP - HID), (0, 0)))),
        "b2": _f(inputs["L0_ffn_b2"]),
        "f_ln_g": _f(inputs["L0_ffn_ln_g"]),
        "f_ln_b": _f(inputs["L0_ffn_ln_b"]),
        "identb": _bfn(np.eye(128)),
        "onescol": _bfn(np.ones((128, 1))),
        "onesrow": _bfn(np.ones((1, 128))),
    }
    qs = np.asarray(inputs["L0_ffn_q_scale"], np.float32)
    ks = np.asarray(inputs["L0_ffn_k_scale"], np.float32)
    ri = np.concatenate([np.arange(32, 64), np.arange(0, 32)])
    pat = np.zeros((2, 128, 256), np.float32)
    for l in range(2):
        row = np.concatenate([qs[l], qs[l][ri], ks[l], ks[l][ri]])
        pat[l] = row[None, :]
    base["qkpat"] = pat
    estr = np.zeros((KC, 128, H), np.float32)
    for k in range(KC):
        for p in range(128):
            estr[k, p, (128 * k + p) // HD] = 1.0
    base["estrip"] = _bfn(estr)

    in_maps = []
    for c in range(8):
        b, q = c // 4, c % 4
        t0 = q * T
        im = dict(base)
        im["xin"] = _f(x[b, t0:t0 + T])
        im["rotpk"] = _ROT[t0:t0 + T].copy()
        im["maskt"] = _masks(t0)
        im["qzin"] = _f(np.full((128, 1), 0.0 if q == 0 else 1.0))
        prev = c - 1 if q > 0 else c
        im["poffk"] = np.array([[prev * AGKN]], np.int32)
        im["poffc"] = np.array([[prev * AGCN]], np.int32)
        sa = np.zeros((H, 8), np.float32)
        for sl in range(8):
            if (sl // 4 == b) and (sl % 4 < q):
                sa[:, sl] = 1.0
        im["selAd"] = sa
        im["selAid"] = 1.0 - sa
        im["selBd"] = np.repeat(sa, HD, axis=1)
        in_maps.append(im)
    return in_maps


def kernel(**inputs):
    nc, _ = build_program()
    in_maps = make_inputs(inputs)
    res = run_bass_kernel_spmd(nc, in_maps, core_ids=list(range(8)))
    outp = np.zeros((B, S, D), np.float32)
    for c in range(8):
        b, q = c // 4, c % 4
        outp[b, q * T:(q + 1) * T] = res.results[c]["out"]
    return outp


# version 20
# speedup vs baseline: 1.0388x; 1.0388x over previous
import contextlib
import numpy as np
import ml_dtypes
import concourse.bass as bass
import concourse.mybir as mybir
from concourse.bass_utils import run_bass_kernel_spmd

F32 = mybir.dt.float32
BF16 = mybir.dt.bfloat16
I32 = mybir.dt.int32
AF = mybir.ActivationFunctionType
ALU = mybir.AluOpType
AX = mybir.AxisListType

B, S, D = 2, 8192, 1024
T = 2048
NT = 16
KC = 8
H, HD, CS = 16, 64, 16
NCH = T // CS
HEADS, DH, WIN = 8, 64, 512
INNER = 512
SPAN = 640
HID = 2730
HIDP = 2816
KH = 22
EPS = 1e-5
AG1N = 16 + 1024
AGKN = 8 * 128 * 512 + 1024
AGCN = 1024
WSL = 512
W1GW = [512, 512, 512, 512, 512, 170]
W1G0 = [0, 512, 1024, 1536, 2048, 2560]
NW1G = 6
AR = 21576


def _f(x):
    return np.ascontiguousarray(np.asarray(x), dtype=np.float32)


def _bfn(x):
    return np.ascontiguousarray(np.asarray(x, dtype=np.float32).astype(ml_dtypes.bfloat16))


class Ct:
    def __init__(self, sem, step=1):
        self.sem, self.step, self.n = sem, step, 0

    def inc(self, instr):
        instr.then_inc(self.sem, self.step)
        self.n += self.step

    def wait(self, eng):
        if self.n > 0:
            eng.wait_ge(self.sem, self.n)


def pbc(row_ap, n):
    dims = [[1, 1], [0, n]] + [list(d) for d in row_ap.ap[1:]]
    return bass.AP(row_ap.tensor, row_ap.offset, dims)


def rotread(ap2d, nh, hd):
    ps = ap2d.ap[0][0]
    half = hd // 2
    return bass.AP(ap2d.tensor, ap2d.offset + half,
                   [[ps, ap2d.ap[0][1]], [hd, nh], [-half, 2], [1, half]])


def build_program(dbg=()):
    nc = bass.Bass("TRN2", target_bir_lowering=False)
    dbg = set(dbg)
    taps = []

    def din(name, shape, dt=F32):
        return nc.dram_tensor(name, shape, dt, kind="ExternalInput")

    xin = din("xin", [T, D])
    w_inT = din("w_inT", [D, D])
    w_gateT = din("w_gateT", [D, 2 * H])
    a_ln_g = din("a_ln_g", [D])
    a_ln_b = din("a_ln_b", [D])
    bgate = din("bgate", [2 * H])
    eigr = din("eigr", [H])
    w_outT = din("w_outT", [D, D])
    wqkv = din("wqkv", [2, D, 3 * INNER])
    wo = din("wo", [2, INNER, D])
    w1 = din("w1", [2, D, 2 * HID])
    b1 = din("b1", [2, 2 * HID])
    w2p = din("w2p", [2, HIDP, D])
    b2 = din("b2", [2, D])
    f_ln_g = din("f_ln_g", [2, D])
    f_ln_b = din("f_ln_b", [2, D])
    qkpat = din("qkpat", [2, 128, 256])
    rotpk = din("rotpk", [T, 256])
    maskt = din("maskt", [NT, 128, SPAN], BF16)
    identb = din("identb", [128, 128], BF16)
    onescol = din("onescol", [128, 1], BF16)
    onesrow = din("onesrow", [1, 128], BF16)
    estrip = din("estrip", [KC, 128, H], BF16)
    selAd = din("selAd", [H, 8])
    selAid = din("selAid", [H, 8])
    selBd = din("selBd", [H, 8 * HD])
    qzin = din("qzin", [128, 1])
    poffk = din("poffk", [1, 1], I32)
    poffc = din("poffc", [1, 1], I32)

    out = nc.dram_tensor("out", [T, D], F32, kind="ExternalOutput")
    x1sp = nc.dram_tensor("x1sp", [T, D], F32)
    cbsp = nc.dram_tensor("cbsp", [48, T], F32)
    rwsp = nc.dram_tensor("rwsp", [2, T], F32)
    hgesp = nc.dram_tensor("hgesp", [NT, KH, 128, 128], BF16)
    ag1_in = nc.dram_tensor("ag1_in", [1, AG1N], F32)
    ag1_out = nc.dram_tensor("ag1_out", [8, AG1N], F32)
    agk_in = [nc.dram_tensor(f"agk_in{l}", [1, AGKN], BF16) for l in range(2)]
    agk_out = [nc.dram_tensor(f"agk_out{l}", [8, AGKN], BF16) for l in range(2)]
    agc_in = [nc.dram_tensor(f"agc_in{l}", [1, AGCN], BF16) for l in range(2)]
    agc_out = [nc.dram_tensor(f"agc_out{l}", [8, AGCN], BF16) for l in range(2)]

    est = contextlib.ExitStack()
    with est:
        def sbt(name, shape, dt):
            return est.enter_context(nc.sbuf_tensor(name, shape, dt))

        cur = sbt("cur", [128, NT * D], F32)
        wAr = sbt("wAr", [128, 12288], BF16)
        scr = sbt("scr", [128, 2048], F32)
        cp2 = sbt("cp2", [128, 1536], F32)
        stt = sbt("stt", [128, 80], F32)
        tlb = sbt("tlb", [128, 1024], BF16)
        tlb2 = sbt("tlb2", [128, 1024], BF16)
        dbb = sbt("dbb", [128, 128], BF16)
        ptb = sbt("ptb", [128, SPAN], BF16)
        rtt = sbt("rtt", [128, 256], F32)
        mkt = sbt("mkt", [128, SPAN], BF16)
        zzb = sbt("zzb", [128, 128], BF16)
        sid = sbt("sid", [128, 128], BF16)
        so_c = sbt("so_c", [128, 1], BF16)
        so_r = sbt("so_r", [1, 128], BF16)
        sest = sbt("sest", [128, KC * H], BF16)
        cmat = sbt("cmat", [128, H], BF16)
        lngA = sbt("lngA", [128, KC], F32)
        lnbA = sbt("lnbA", [128, KC], F32)
        lngF = sbt("lngF", [128, KC], F32)
        lnbF = sbt("lnbF", [128, KC], F32)
        lnb_b = sbt("lnb_b", [128, KC], BF16)
        lnb_lo = sbt("lnb_lo", [128, KC], BF16)
        qkp = sbt("qkp", [128, 256], F32)
        sqz = sbt("sqz", [128, 1], F32)
        svc = sbt("svc", [16, D], BF16)
        pcol = sbt("pcol", [128, KC], BF16)
        sb1g = sbt("sb1g", [1, 1536], BF16)
        sb2r = sbt("sb2r", [1, D], BF16)
        soffk = sbt("soffk", [1, 1], I32)
        soffc = sbt("soffc", [1, 1], I32)
        seps = sbt("seps", [128, 1], F32)
        pib = sbt("pib", [16, 2048], BF16)
        arena = sbt("arena", [128, AR], F32)

        def arf(off, n, p0=0, p1=128):
            assert off + n <= AR, (off, n)
            return arena[p0:p1, off:off + n]

        def arb(off, n, p0=0, p1=128):
            assert off + (n + 1) // 2 <= AR, (off, n)
            return arena[p0:p1, off:off + (n + 1) // 2].bitcast(BF16)

        stripA = [arb(k * 1025, 1 + T) for k in range(KC)]
        sigv = arf(0, 2048, 0, 32)
        acmp = arf(0, 2048, 0, 16)
        betav = arf(0, 2048, 16, 32)
        cumA = arf(2048, 2048, 0, 16)
        cumB = arf(2048 + 2048, 2048, 0, 16)
        aindv = arf(6144, 2048, 0, 16)
        stripB = [arb(8200 + k * 1024, T) for k in range(KC)]
        kts = [arb(8200 + j * 1280, WIN + T) for j in range(4)]
        vts = arb(13320, (NT + 4) * INNER)
        BC1 = arf(16392, 1024)
        BFz = arf(17416, 1024)
        BC2 = arf(18440, 1024)
        Qs = arf(19464, 1024)
        Vs = arf(20488, 1024)
        qtt = arb(18440, 512)
        qtt5 = arb(20296, 5 * 512)
        qtt = arb(18440, 512)
        qtt5 = arb(18696, 5 * 512)
        loTt = arb(18440, 1024)
        graw = arf(16392, 2048, 0, 32)
        wg32 = arf(19464, 256)
        wg_hi = arb(19720, 256)
        wg_lo = arb(19848, 256)
        smask = arf(19976, 1280)
        pbuf = arb(21256, SPAN)
        pbuf2 = arb(19976, SPAN)
        dbb2 = arb(20296, 128)
        w2h = arb(8200, KH * 512)
        sqs = arb(13832, 1 + T)
        bcr1 = arf(14857, 1024)
        bcr2 = arf(15881, 1024)
        hgt = arb(16905, HIDP)
        bscr = arf(18313, 1024)

        CP_SELB = 0
        CP_BALL = 512
        CP_SELA = 1024
        CP_SELAI = 1032
        CP_EIG = 1040
        CP_BG = 1041
        CP_C2 = 1042
        CP_AALL = 1043
        CP_R = 1051
        CP_CST = 1115
        CP_PUB = 1123
        CP_CHA = 1124
        CP_CHB = 1252

        def pst(name, shape, dt):
            return est.enter_context(nc.psum_tensor(name, shape, dt))

        PTR = pst("PTR", [128, 1024], BF16)
        PQ = pst("PQ", [128, 1536], F32)
        PS0 = pst("PS0", [128, 640], F32)
        PS1 = pst("PS1", [128, 640], F32)

        sd = Ct(est.enter_context(nc.semaphore("sd")), 16)
        sg = Ct(est.enter_context(nc.semaphore("sg")), 16)
        sv = Ct(est.enter_context(nc.semaphore("sv")), 1)
        ss = Ct(est.enter_context(nc.semaphore("ss")), 1)
        stn = Ct(est.enter_context(nc.semaphore("stn")), 1)
        sq = Ct(est.enter_context(nc.semaphore("sq")), 1)
        scc = Ct(est.enter_context(nc.semaphore("scc")), 1)
        fin = Ct(est.enter_context(nc.semaphore("fin")), 16)
        ALLC = [sd, sg, sv, ss, stn, sq, scc]

        def VI(v, instr):
            sv.inc(instr)
            v.wait_ge(sv.sem, sv.n)

        def SI(sc, instr):
            ss.inc(instr)
            sc.wait_ge(ss.sem, ss.n)

        _BQ = []

        def flushb():
            if not _BQ:
                return
            with nc.Block(no_gpsimd_drain=True) as b:
                for en, fn in _BQ:
                    getattr(b, en)(fn)
            _BQ.clear()

        def blk(*fns):
            _BQ.extend(fns)
            if len(_BQ) >= 48:
                flushb()

        def wall(eng):
            for c in ALLC:
                c.wait(eng)

        def tap(name, apf, shape, dt=F32):
            if name not in dbg:
                return
            dt_ = nc.dram_tensor("dbg_" + name, shape, dt, kind="ExternalOutput")

            def _t(s):
                wall(s)
                sd.inc(s.dma_start(out=dt_.ap(), in_=apf()))
            blk(("sync", _t))

            def _w(e):
                sd.wait(e)
            blk(("vector", _w), ("scalar", _w), ("tensor", _w), ("gpsimd", _w))
            taps.append("dbg_" + name)

        wv = lambda a, b_: wAr[:, a:b_]
        wvin = lambda k: wv(k * 1024, (k + 1) * 1024)
        wvg = lambda k: wv(8192 + k * 32, 8192 + (k + 1) * 32)
        wvq = lambda k: wv(k * 1536, (k + 1) * 1536)
        wvo = lambda j: wv(j * 1024, (j + 1) * 1024)

        def s_setup(s):
            dst = cur[:, 0:8 * D].rearrange("p (i c) -> p i c", c=D)
            src = bass.AP(xin, 0, [[D, 128], [128 * D, 8], [1, D]])
            sd.inc(s.dma_start(out=dst, in_=src))
            sd.inc(s.dma_start(out=sid[:], in_=identb.ap()))
            sd.inc(s.dma_start(out=so_c[:], in_=onescol.ap()))
            sd.inc(s.dma_start(out=so_r[:], in_=onesrow.ap()))
            sd.inc(s.dma_start(out=sest[:].rearrange("p (k h) -> p k h", h=H),
                               in_=bass.AP(estrip, 0, [[H, 128], [128 * H, KC], [1, H]])))
            sd.inc(s.dma_start(out=lngA[:], in_=bass.AP(a_ln_g, 0, [[1, 128], [128, KC]]), allow_slow_non_contiguous=True))
            sd.inc(s.dma_start(out=lnbA[:], in_=bass.AP(a_ln_b, 0, [[1, 128], [128, KC]]), allow_slow_non_contiguous=True))
            sd.inc(s.dma_start(out=sqz[:], in_=qzin.ap()))
            sd.inc(s.dma_start(out=soffk[:], in_=poffk.ap()))
            sd.inc(s.dma_start(out=soffc[:], in_=poffc.ap()))
            sd.inc(s.dma_start(out=cp2[0:32, CP_BG:CP_BG + 1],
                               in_=bass.AP(bgate, 0, [[1, 32], [1, 1]])))
            sd.inc(s.dma_start(
                out=wg32[:].rearrange("p (k c) -> p k c", c=2 * H),
                in_=bass.AP(w_gateT, 0, [[2 * H, 128], [128 * 2 * H, KC],
                                         [1, 2 * H]])))
            sd.inc(s.dma_start(out=cp2[0:16, CP_EIG:CP_EIG + 1],
                               in_=bass.AP(eigr, 0, [[1, 16], [1, 1]])))
            sd.inc(s.dma_start(out=cp2[0:16, CP_SELA:CP_SELA + 8], in_=selAd.ap()))
            sd.inc(s.dma_start(out=cp2[0:16, CP_SELAI:CP_SELAI + 8], in_=selAid.ap()))
            sd.inc(s.dma_start(out=cp2[0:16, CP_SELB:CP_SELB + 512], in_=selBd.ap()))

        def s_setup_g(g):
            dst2 = cur[:, 8 * D:16 * D].rearrange("p (i c) -> p i c", c=D)
            src2 = bass.AP(xin, 8 * 128 * D, [[D, 128], [128 * D, 8], [1, D]])
            sg.inc(g.dma_start(out=dst2, in_=src2))
            sg.inc(g.dma_start(out=wv(0, 8192).rearrange("p (k c) -> p k c", c=D),
                               in_=bass.AP(w_inT, 0, [[D, 128], [128 * D, KC], [1, D]])))
            sg.inc(g.dma_start(
                out=wv(8192, 8448).rearrange("p (k c) -> p k c", c=32),
                in_=bass.AP(w_gateT, 0, [[2 * H, 128], [128 * 2 * H, KC], [1, 2 * H]])))

        blk(("sync", s_setup), ("gpsimd", s_setup_g))

        def s_pre(sc):
            sd.wait(sc)
            SI(sc, sc.activation(cp2[0:16, CP_EIG:CP_EIG + 1],
                                 cp2[0:16, CP_EIG:CP_EIG + 1], AF.Tanh))
            SI(sc, sc.copy(lnb_b[:], lnbA[:]))
            SI(sc, sc.copy(wg_hi[:], wg32[:]))

        def s_pre_v(v):
            VI(v, v.memset(zzb[:], 0.0))
            VI(v, v.memset(seps[:], EPS))
            ss.wait(v)
            VI(v, v.tensor_tensor(lnb_lo[:], lnbA[:], lnb_b[:], ALU.subtract))
            VI(v, v.tensor_tensor(wg_lo[:], wg32[:], wg_hi[:], ALU.subtract))

        blk(("scalar", s_pre), ("vector", s_pre_v))

        def emit_ln_stats():
            def f_v(v):
                sd.wait(v)
                sg.wait(v)
                sv.wait(v)
                for i in range(NT):
                    VI(v, v.reduce_sum(stt[:, i:i + 1], cur[:, i * D:(i + 1) * D],
                                        AX.X))

            def f_s(sc):
                sd.wait(sc)
                sg.wait(sc)
                sv.wait(sc)
                for i in range(NT):
                    SI(sc, sc.activation(scr[:, 0:1024], cur[:, i * D:(i + 1) * D],
                                         AF.Square, accum_out=stt[:, 16 + i:17 + i]))

            blk(("vector", f_v), ("scalar", f_s))

            def f_v1(v):
                ss.wait(v)
                VI(v, v.tensor_scalar_mul(stt[:, 32:48], stt[:, 0:16], 1.0 / D))
                VI(v, v.tensor_tensor(stt[:, 64:80], stt[:, 32:48], stt[:, 32:48],
                                       ALU.mult))
                VI(v, v.tensor_scalar_mul(stt[:, 48:64], stt[:, 16:32], 1.0 / D))
                VI(v, v.tensor_tensor(stt[:, 48:64], stt[:, 48:64], stt[:, 64:80],
                                       ALU.subtract))

            def f_s2(sc):
                sv.wait(sc)
                SI(sc, sc.activation(stt[:, 48:64], stt[:, 48:64], AF.Sqrt, bias=seps[:]))

            def f_v2(v):
                ss.wait(v)
                VI(v, v.reciprocal(stt[:, 48:64], stt[:, 48:64]))

            blk(("vector", f_v1))
            blk(("scalar", f_s2))
            blk(("vector", f_v2))

        def emit_tile_transpose(strips, col0, i, src=None):
            srcf = src or (lambda k: tlb[:, k * 128:(k + 1) * 128])

            def t_f(te):
                sv.wait(te)
                ss.wait(te)
                for k in range(KC):
                    ins = te.transpose(PTR[:, k * 128:(k + 1) * 128], srcf(k), sid[:])
                stn.inc(ins)

            def s_f(v):
                stn.wait(v)
                ins = None
                for k in range(KC):
                    ins = v.tensor_copy(strips[k][:, col0 + i * 128: col0 + (i + 1) * 128],
                                        PTR[:, k * 128:(k + 1) * 128])
                ss.inc(ins)
            blk(("tensor", t_f), ("vector", s_f))

        tap("x0", lambda: cur[:, 0:1024], [128, 1024])
        emit_ln_stats()
        tap("st0", lambda: stt[:], [128, 80])

        for i in range(NT):
            def a2_v(v, i=i):
                ss.wait(v)
                VI(v, v.tensor_scalar(tlb[:], cur[:, i * D:(i + 1) * D],
                                       stt[:, 32 + i:33 + i], stt[:, 48 + i:49 + i],
                                       ALU.subtract, ALU.mult))
            blk(("vector", a2_v))
            if i == 0:
                tap("tl0", lambda: tlb[:], [128, 1024], BF16)
            emit_tile_transpose(stripA, 1, i)

        tap("ln0T", lambda: stripA[0][:, 1:1 + T], [128, T], BF16)

        def a4_t(te):
            ss.wait(te)
            sv.wait(te)
            nmm = 0
            for (lv, wv_) in ((lnb_b, wg_hi), (lnb_b, wg_lo), (lnb_lo, wg_hi)):
                for k in range(KC):
                    ins = te.matmul(PS0[0:1, 0:32], lv[:, k:k + 1],
                                    wv_[:, k * 2 * H:(k + 1) * 2 * H],
                                    start=(nmm == 0), stop=(nmm == 23))
                    nmm += 1
            stn.inc(ins)

        def a4_v(v):
            stn.wait(v)
            VI(v, v.tensor_copy(scr[0:1, 0:32], PS0[0:1, 0:32]))

        blk(("tensor", a4_t), ("vector", a4_v))

        def a4_s(s):
            sv.wait(s)
            sd.inc(s.dma_start(out=cp2[0:32, CP_C2:CP_C2 + 1], in_=scr[0:1, 0:32]))

        blk(("sync", a4_s))

        def a4_v2(v):
            sd.wait(v)
            stn.wait(v)
            VI(v, v.tensor_tensor(cp2[0:32, CP_C2:CP_C2 + 1],
                                   cp2[0:32, CP_C2:CP_C2 + 1],
                                   cp2[0:32, CP_BG:CP_BG + 1], ALU.add))
            for k in range(KC):
                VI(v, v.tensor_scalar_mul(wg32[:, k * 2 * H:(k + 1) * 2 * H],
                                          wg32[:, k * 2 * H:(k + 1) * 2 * H],
                                          lngA[:, k:k + 1]))
            VI(v, v.tensor_copy(wg_hi[:], wg32[:]))
            VI(v, v.tensor_tensor(wg_lo[:], wg32[:], wg_hi[:], ALU.subtract))

        blk(("vector", a4_v2))


        for i in range(NT):
            def a3_t(te, i=i):
                ss.wait(te)
                sg.wait(te)
                sv.wait(te)
                for nb in range(2):
                    for k in range(KC):
                        ins = te.matmul(PQ[:, nb * 512:(nb + 1) * 512],
                                        stripA[k][:, 1 + i * 128: 1 + (i + 1) * 128],
                                        wvin(k)[:, nb * 512:(nb + 1) * 512],
                                        start=(k == 0), stop=(k == KC - 1))
                stn.inc(ins)

            def a3_v(v, i=i):
                stn.wait(v)
                ss.wait(v)
                VI(v, v.bn_stats(stt[:, 8:14], PQ[:, 0:512]))
                VI(v, v.bn_stats(stt[:, 14:20], PQ[:, 512:1024]))
                VI(v, v.bn_aggr(stt[:, 2:4], stt[:, 8:20]))

            def a3_s2(sc, i=i):
                sv.wait(sc)
                SI(sc, sc.activation(stt[:, 3:4], stt[:, 3:4], AF.Sqrt, bias=seps[:]))

            def a3_v3(v, i=i):
                ss.wait(v)
                VI(v, v.reciprocal(stt[:, 3:4], stt[:, 3:4]))
                VI(v, v.tensor_scalar(scr[:, 1024:2048], PQ[:, 0:1024],
                                       stt[:, 2:3], stt[:, 3:4],
                                       ALU.subtract, ALU.mult))
                VI(v, v.tensor_copy(tlb[:], scr[:, 1024:2048]))
                VI(v, v.tensor_tensor(tlb2[:], scr[:, 1024:2048], tlb[:],
                                       ALU.subtract))

            blk(("tensor", a3_t), ("vector", a3_v))
            blk(("scalar", a3_s2))
            blk(("vector", a3_v3))
            emit_tile_transpose(stripB, 0, i)

            def a3_tlo(te, i=i):
                sv.wait(te)
                ss.wait(te)
                for k in range(KC):
                    ins = te.transpose(PTR[:, k * 128:(k + 1) * 128],
                                       tlb2[:, k * 128:(k + 1) * 128], sid[:])
                stn.inc(ins)

            def a3_slo(v, i=i):
                stn.wait(v)
                ss.inc(v.tensor_copy(loTt[:], PTR[:]))
                v.wait_ge(ss.sem, ss.n)

            blk(("tensor", a3_tlo), ("vector", a3_slo))

            def a3_tg(te, i=i):
                ss.wait(te)
                nmm = 0
                for (lhs_w, rhs_t) in ((wg_hi, 0), (wg_lo, 0), (wg_hi, 1)):
                    for k in range(KC):
                        rhs = (stripB[k][:, i * 128:(i + 1) * 128] if rhs_t == 0
                               else loTt[:, k * 128:(k + 1) * 128])
                        ins = te.matmul(PS1[0:32, 0:128],
                                        lhs_w[:, k * 2 * H:(k + 1) * 2 * H], rhs,
                                        start=(nmm == 0), stop=(nmm == 23))
                        nmm += 1
                stn.inc(ins)

            def a3_sg(v, i=i):
                stn.wait(v)
                ss.inc(v.tensor_copy(graw[:, i * 128:(i + 1) * 128],
                                     PS1[0:32, 0:128]))
                v.wait_ge(ss.sem, ss.n)

            blk(("tensor", a3_tg), ("vector", a3_sg))

        tap("xplT", lambda: stripB[0][:, 0:T], [128, T], BF16)

        for cb in range(4):
            def a5_s(sc, cb=cb):
                ss.wait(sc)
                sv.wait(sc)
                SI(sc, sc.activation(sigv[:, cb * 512:(cb + 1) * 512],
                                     graw[:, cb * 512:(cb + 1) * 512], AF.Sigmoid,
                                     bias=cp2[0:32, CP_C2:CP_C2 + 1]))

            blk(("scalar", a5_s))

        def a5_v(v):
            ss.wait(v)
            VI(v, v.tensor_scalar_mul(acmp[:], acmp[:],
                                       cp2[0:16, CP_EIG:CP_EIG + 1]))

        blk(("vector", a5_v))
        tap("acmp", lambda: acmp[:], [16, T])
        tap("beta", lambda: betav[:], [16, T])

        def a6_v(v):
            sv.wait(v)
            a3_ = acmp.rearrange("p (n i) -> p n i", i=CS)
            c3 = cumA.rearrange("p (n i) -> p n i", i=CS)
            VI(v, v.tensor_copy(c3[:, :, 0:1], a3_[:, :, 0:1]))
            for ii in range(1, CS):
                VI(v, v.tensor_tensor(c3[:, :, ii:ii + 1], c3[:, :, ii - 1:ii],
                                      a3_[:, :, ii:ii + 1], ALU.mult))
            VI(v, v.tensor_tensor(cumB[:], cumA[:], cumA[:], ALU.mult))
            sq3 = cumB.rearrange("p (n i) -> p n i", i=CS)
            i3 = aindv.rearrange("p (n i) -> p n i", i=CS)
            VI(v, v.tensor_single_scalar(i3[:, :, 1:CS], sq3[:, :, 0:CS - 1],
                                          1e-16, ALU.is_gt))
            VI(v, v.memset(i3[:, :, 0:1], 1.0))
            VI(v, v.tensor_tensor(aindv[:], aindv[:], acmp[:], ALU.mult))
            VI(v, v.tensor_copy(cp2[0:16, CP_CHA:CP_CHA + NCH], c3[:, :, CS - 1]))
            for stp, sf in enumerate([1, 2, 4, 8, 16, 32, 64]):
                srcr = cp2[0:16, CP_CHA:CP_CHA + NCH] if stp % 2 == 0 \
                    else cp2[0:16, CP_CHB:CP_CHB + NCH]
                dstr = cp2[0:16, CP_CHB:CP_CHB + NCH] if stp % 2 == 0 \
                    else cp2[0:16, CP_CHA:CP_CHA + NCH]
                VI(v, v.tensor_copy(dstr[:, 0:sf], srcr[:, 0:sf]))
                VI(v, v.tensor_tensor(dstr[:, sf:NCH], srcr[:, sf:NCH],
                                       srcr[:, 0:NCH - sf], ALU.mult))
            VI(v, v.tensor_copy(cumB[:, 0:CS], cumA[:, 0:CS]))
            pe = cp2[0:16, CP_CHB:CP_CHB + NCH - 1].unsqueeze(2) \
                .broadcast_to([16, NCH - 1, CS])
            p3 = cumB[:, CS:T].rearrange("p (n i) -> p n i", i=CS)
            cc3 = cumA[:, CS:T].rearrange("p (n i) -> p n i", i=CS)
            VI(v, v.tensor_tensor(p3, cc3, pe, ALU.mult))
            VI(v, v.tensor_copy(pib[:], cumB[:]))

        blk(("vector", a6_v))

        def a6_sp(s):
            sv.wait(s)
            sd.inc(s.dma_start(out=bass.AP(cbsp, 0, [[T, 16], [1, T]]), in_=acmp[:]))
            sd.inc(s.dma_start(out=bass.AP(cbsp, 16 * T, [[T, 16], [1, T]]),
                               in_=betav[:]))
            sd.inc(s.dma_start(out=bass.AP(cbsp, 32 * T, [[T, 16], [1, T]]),
                               in_=aindv[:]))

        blk(("sync", a6_sp))
        tap("pit", lambda: cumB[:], [16, T])
        tap("aind", lambda: aindv[:], [16, T])

        for k in range(KC):
            h0 = 2 * k

            def a7_s(v, k=k):
                ss.wait(v)
                sd.wait(v)
                VI(v, v.tensor_scalar(scr[:], stripB[k][:, 0:T],
                                       lngA[:, k:k + 1], lnbA[:, k:k + 1],
                                       ALU.mult, ALU.add))

            blk(("vector", a7_s))

            for hf in range(2):
                o = hf * 1024

                def a7_d1(s, k=k, h0=h0, hf=hf, o=o):
                    sv.wait(s)
                    s.wait_ge(sd.sem, sd.n)
                    sd.inc(s.dma_start(out=BC1[0:64, :],
                                       in_=bass.AP(cbsp, (16 + h0) * T + o,
                                                   [[0, 64], [1, 1024]])))
                    sd.inc(s.dma_start(out=BC1[64:128, :],
                                       in_=bass.AP(cbsp, (17 + h0) * T + o,
                                                   [[0, 64], [1, 1024]])))
                    sd.inc(s.dma_start(out=BC2[0:64, :],
                                       in_=bass.AP(cbsp, (32 + h0) * T + o,
                                                   [[0, 64], [1, 1024]])))
                    sd.inc(s.dma_start(out=BC2[64:128, :],
                                       in_=bass.AP(cbsp, (33 + h0) * T + o,
                                                   [[0, 64], [1, 1024]])))

                def a7_v1(v, k=k, hf=hf, o=o):
                    ss.wait(v)
                    sd.wait(v)
                    VI(v, v.tensor_tensor(BFz, scr[:, o:o + 1024], BC1[:], ALU.mult))
                    VI(v, v.tensor_tensor(Qs, BFz, BC2[:], ALU.mult))
                    q3 = Qs.rearrange("p (n i) -> p n i", i=CS)
                    b3 = BFz.rearrange("p (n i) -> p n i", i=CS)
                    nh = 1024 // CS
                    if hf == 1:
                        pass
                    VI(v, v.tensor_tensor(stt[:, 0:64], b3[:, :, CS - 1],
                                           q3[:, :, CS - 1], ALU.subtract))
                    VI(v, v.tensor_tensor(BFz, BFz, Qs, ALU.subtract))

                def a7_d2(s, k=k, h0=h0, o=o):
                    sv.wait(s)
                    sd.inc(s.dma_start(out=BC2[0:64, :],
                                       in_=bass.AP(cbsp, h0 * T + o,
                                                   [[0, 64], [1, 1024]])))
                    sd.inc(s.dma_start(out=BC2[64:128, :],
                                       in_=bass.AP(cbsp, (h0 + 1) * T + o,
                                                   [[0, 64], [1, 1024]])))

                def a7_v2(v, k=k, hf=hf, o=o):
                    sd.wait(v)
                    nh = 1024 // CS
                    q3 = Qs.rearrange("p (n i) -> p n i", i=CS)
                    a3 = BC2.rearrange("p (n i) -> p n i", i=CS)
                    VI(v, v.tensor_tensor(Vs[:, 0:nh - 1], stt[:, 0:nh - 1],
                                           a3[:, 1:nh, 0], ALU.mult))
                    VI(v, v.tensor_tensor(q3[:, 1:nh, 0], q3[:, 1:nh, 0],
                                           Vs[:, 0:nh - 1], ALU.add))
                    if hf == 1:
                        VI(v, v.tensor_tensor(Vs[:, 100:101], stt[:, 64:65],
                                               a3[:, 0:1, 0], ALU.mult))
                        VI(v, v.tensor_tensor(q3[:, 0:1, 0], q3[:, 0:1, 0],
                                               Vs[:, 100:101], ALU.add))
                    VI(v, v.tensor_copy(stt[:, 64:65], stt[:, 63:64]))
                    if hf == 0:
                        VI(v, v.tensor_tensor_scan(Vs, BC2[:], Qs, 0.0,
                                                    ALU.mult, ALU.add))
                    else:
                        VI(v, v.tensor_tensor_scan(Vs, BC2[:], Qs, stt[:, 65:66],
                                                    ALU.mult, ALU.add))
                    VI(v, v.tensor_copy(stt[:, 65:66], Vs[:, 1023:1024]))
                    VI(v, v.tensor_tensor(stripB[k][:, o:o + 1024], BFz, Vs,
                                           ALU.add))
                    if hf == 1:
                        VI(v, v.tensor_tensor(cp2[:, CP_PUB:CP_PUB + 1],
                                               BFz[:, 1023:1024], Vs[:, 1023:1024],
                                               ALU.add))

                blk(("sync", a7_d1))
                blk(("vector", a7_v1))
                blk(("sync", a7_d2))
                blk(("vector", a7_v2))

            def a7_st(s, k=k):
                sv.wait(s)
                sd.inc(s.dma_start(out=bass.AP(ag1_in, 16 + 128 * k, [[1, 1], [1, 128]]),
                                   in_=cp2[:, CP_PUB:CP_PUB + 1]))

            blk(("sync", a7_st))

        tap("h0T", lambda: stripB[0][:, 0:T], [128, T], BF16)

        def a7_pubA(s):
            sv.wait(s)
            sd.inc(s.dma_start(out=bass.AP(ag1_in, 0, [[1, 1], [1, 16]]),
                               in_=cp2[0:16, CP_CHB + NCH - 1:CP_CHB + NCH]))

        blk(("sync", a7_pubA))

        flushb()

        def ag1_g(g):
            sd.wait(g)
            scc.inc(g.collective_compute(
                "AllGather", ALU.bypass, replica_groups=[list(range(8))],
                ins=[ag1_in.ap()], outs=[ag1_out.ap()]))

        blk(("gpsimd", ag1_g))

        def a8_wg(g):
            stn.wait(g)
            sg.inc(g.dma_start(out=wv(0, 8192).rearrange("p (k c) -> p k c", c=D),
                               in_=bass.AP(w_outT, 0, [[D, 128], [128 * D, KC], [1, D]])))

        blk(("gpsimd", a8_wg))

        def a8_d(s):
            scc.wait(s)
            sd.inc(s.dma_start(out=cp2[0:16, CP_AALL:CP_AALL + 8],
                               in_=bass.AP(ag1_out, 0, [[1, 16], [AG1N, 8]]), allow_slow_non_contiguous=True))
            sd.inc(s.dma_start(out=cp2[0:16, CP_BALL:CP_BALL + 512]
                               .rearrange("p (s d) -> p s d", d=HD),
                               in_=bass.AP(ag1_out, 16, [[64, 16], [AG1N, 8], [1, 64]])))

        def a8_v(v):
            sd.wait(v)
            A_ = cp2[0:16, CP_AALL:CP_AALL + 8]
            VI(v, v.tensor_tensor(A_, A_, cp2[0:16, CP_SELA:CP_SELA + 8], ALU.mult))
            VI(v, v.tensor_tensor(A_, A_, cp2[0:16, CP_SELAI:CP_SELAI + 8], ALU.add))
            B_ = cp2[0:16, CP_BALL:CP_BALL + 512]
            VI(v, v.tensor_tensor(B_, B_, cp2[0:16, CP_SELB:CP_SELB + 512], ALU.mult))
            R_ = cp2[0:16, CP_R:CP_R + HD]
            VI(v, v.memset(R_, 0.0))
            for sl in range(8):
                VI(v, v.scalar_tensor_tensor(
                    R_, R_, cp2[0:16, CP_AALL + sl:CP_AALL + sl + 1],
                    cp2[0:16, CP_BALL + sl * HD:CP_BALL + (sl + 1) * HD],
                    ALU.mult, ALU.add))

        blk(("sync", a8_d), ("vector", a8_v))
        tap("carry", lambda: cp2[0:16, CP_R:CP_R + HD], [16, HD])

        def a8_d2(s):
            sv.wait(s)
            for k in range(KC):
                sd.inc(s.dma_start(out=cp2[:, CP_CST + k:CP_CST + k + 1],
                                   in_=cp2[2 * k:2 * k + 2, CP_R:CP_R + HD]))

        blk(("sync", a8_d2))

        for nb in range(2):
            for k in range(KC):
                def a8_cm(v, k=k, nb=nb):
                    if nb == 0 and k == 0:
                        sd.wait(v)
                    ss.wait(v)
                    VI(v, v.tensor_scalar_mul(cmat[:], sest[:, k * H:(k + 1) * H],
                                               cp2[:, CP_CST + k:CP_CST + k + 1]))

                def a8_mm(te, k=k, nb=nb):
                    sv.wait(te)
                    sg.wait(te)
                    ins = te.matmul(PS1[0:16, 0:512], cmat[:],
                                    wvin(k)[:, nb * 512:(nb + 1) * 512],
                                    start=(k == 0), stop=(k == KC - 1))
                    stn.inc(ins)

                blk(("vector", a8_cm), ("tensor", a8_mm))

            def a8_ev(v, nb=nb):
                stn.wait(v)
                ss.inc(v.tensor_copy(svc[:, nb * 512:(nb + 1) * 512],
                                     PS1[0:16, 0:512]))
                v.wait_ge(ss.sem, ss.n)

            blk(("vector", a8_ev))

        tap("vc", lambda: svc[:], [16, D], BF16)

        for i in range(NT):
            c0 = i * 128

            def a9_t(te, i=i, c0=c0):
                ss.wait(te)
                sg.wait(te)
                sv.wait(te)
                for nb in range(2):
                    for k in range(KC):
                        te.matmul(PQ[:, nb * 512:(nb + 1) * 512],
                                  stripB[k][:, i * 128:(i + 1) * 128],
                                  wvin(k)[:, nb * 512:(nb + 1) * 512],
                                  start=(k == 0), stop=False)
                    ins = te.matmul(PQ[:, nb * 512:(nb + 1) * 512],
                                    pib[:, c0:c0 + 128],
                                    svc[:, nb * 512:(nb + 1) * 512],
                                    start=False, stop=True)
                stn.inc(ins)

            def a9_v(v, i=i):
                stn.wait(v)
                ss.wait(v)
                VI(v, v.bn_stats(stt[:, 8:14], PQ[:, 0:512]))
                VI(v, v.bn_stats(stt[:, 14:20], PQ[:, 512:1024]))
                VI(v, v.bn_aggr(stt[:, 2:4], stt[:, 8:20]))

            def a9_s2(sc, i=i):
                sv.wait(sc)
                SI(sc, sc.activation(stt[:, 3:4], stt[:, 3:4], AF.Sqrt, bias=seps[:]))

            def a9_v3(v, i=i):
                ss.wait(v)
                VI(v, v.reciprocal(stt[:, 3:4], stt[:, 3:4]))
                VI(v, v.tensor_scalar(scr[:, 0:1024], PQ[:, 0:1024], stt[:, 2:3],
                                       stt[:, 3:4], ALU.subtract, ALU.mult))
                VI(v, v.tensor_tensor(cur[:, i * D:(i + 1) * D],
                                       cur[:, i * D:(i + 1) * D], scr[:, 0:1024],
                                       ALU.add))

            def a9_sp(s, i=i):
                sv.wait(s)
                sd.inc(s.dma_start(out=bass.AP(x1sp, i * 128 * D, [[D, 128], [1, D]]),
                                   in_=cur[:, i * D:(i + 1) * D]))

            blk(("tensor", a9_t), ("vector", a9_v))
            blk(("scalar", a9_s2))
            blk(("vector", a9_v3), ("sync", a9_sp))

        tap("x1", lambda: cur[:, 0:D], [128, D])

        emit_ln_stats()

        def a10_v(v):
            ss.wait(v)
            sd.wait(v)
            for i in range(NT):
                VI(v, v.tensor_scalar(cur[:, i * D:(i + 1) * D],
                                       cur[:, i * D:(i + 1) * D],
                                       stt[:, 32 + i:33 + i], stt[:, 48 + i:49 + i],
                                       ALU.subtract, ALU.mult))

        blk(("vector", a10_v))
        tap("lnpre", lambda: cur[:, 0:D], [128, D])
        tap("lnpre_all", lambda: cur[:], [128, NT * D])

        def qsl(i):
            return (qtt5[:, (i) * 512:(i + 1) * 512] if i < 5 else qtt)

        def emit_attn(i, L):
            key0 = i * 128

            def s_mk(s, i=i):
                ss.wait(s)
                sv.wait(s)
                sd.inc(s.dma_start(out=mkt[:],
                                   in_=bass.AP(maskt, i * 128 * SPAN,
                                               [[SPAN, 128], [1, SPAN]])))

            blk(("sync", s_mk))

            for j in range(4):
                def t_sim(te, i=i, j=j):
                    ss.wait(te)
                    sd.wait(te)
                    sg.wait(te)
                    sv.wait(te)
                    for hh in range(2):
                        po = hh * 64
                        pd = [PS0, PS1][hh]
                        qs_ = qsl(i)
                        te.matmul(pd[:, 0:512], qs_[po:po + 64, j * 128:(j + 1) * 128],
                                  kts[j][po:po + 64, key0:key0 + 512],
                                  start=True, stop=True)
                        ins = te.matmul(pd[:, 512:SPAN],
                                        qs_[po:po + 64, j * 128:(j + 1) * 128],
                                        kts[j][po:po + 64, key0 + 512:key0 + SPAN],
                                        start=True, stop=True)
                    stn.inc(ins)

                def v_mask(v, i=i, j=j):
                    stn.wait(v)
                    sd.wait(v)
                    ss.wait(v)
                    VI(v, v.tensor_tensor(smask[:, 0:SPAN], PS0[:], mkt[:], ALU.add))
                    VI(v, v.tensor_tensor(smask[:, SPAN:2 * SPAN], PS1[:], mkt[:],
                                           ALU.add))

                def g_max(v, i=i, j=j):
                    m3 = smask[:].rearrange("p (h s) -> p h s", s=SPAN)
                    VI(v, v.tensor_reduce(stt[:, 32:34], m3, AX.X, ALU.max,
                                           negate=True))

                blk(("tensor", t_sim), ("vector", v_mask))
                blk(("vector", g_max))

                def s_exp2(sc, i=i, j=j):
                    sv.wait(sc)
                    stn.wait(sc)
                    SI(sc, sc.activation(
                        pbuf[:, 0:SPAN], smask[:, 0:SPAN], AF.Exp,
                        bias=stt[:, 32:33], accum_out=stt[:, 34:35]))
                    SI(sc, sc.activation(
                        pbuf2[:, 0:SPAN], smask[:, SPAN:2 * SPAN], AF.Exp,
                        bias=stt[:, 33:34], accum_out=stt[:, 35:36]))

                def v_rc2(v, i=i, j=j):
                    ss.wait(v)
                    VI(v, v.reciprocal(stt[:, 36:38], stt[:, 34:36]))
                    VI(v, v.tensor_scalar_mul(dbb[:], sid[:], stt[:, 36:37]))
                    VI(v, v.tensor_scalar_mul(dbb2[:], sid[:], stt[:, 37:38]))

                blk(("scalar", s_exp2))
                blk(("vector", v_rc2))

                for hh in range(2):
                    h = 2 * j + hh
                    pb = pbuf if hh == 0 else pbuf2
                    db = dbb if hh == 0 else dbb2

                    def t_pt(te, i=i, hh=hh, pb=pb, db=db):
                        sv.wait(te)
                        ss.wait(te)
                        for kc in range(5):
                            ins = te.matmul(
                                PS0[:, kc * 128:(kc + 1) * 128],
                                pb[:, kc * 128:(kc + 1) * 128],
                                db[:], start=True, stop=True)
                        stn.inc(ins)

                    def s_pt(v, hh=hh):
                        stn.wait(v)
                        ss.inc(v.tensor_copy(ptb[:], PS0[:, 0:SPAN]))
                        v.wait_ge(ss.sem, ss.n)

                    def t_pv(te, i=i, j=j, hh=hh, h=h):
                        ss.wait(te)
                        for kc in range(5):
                            ins = te.matmul(
                                PQ[hh * 64:(hh + 1) * 64, 1024:1152],
                                vts[:, (i + kc) * INNER + h * DH:
                                    (i + kc) * INNER + (h + 1) * DH],
                                ptb[:, kc * 128:(kc + 1) * 128],
                                start=(kc == 0), stop=(kc == 4))
                        stn.inc(ins)

                    blk(("tensor", t_pt), ("vector", s_pt))
                    blk(("tensor", t_pv))

                def s_av(v, i=i, j=j):
                    stn.wait(v)
                    ss.inc(v.tensor_copy(stripA[j][:, i * 128:(i + 1) * 128],
                                         PQ[:, 1024:1152]))
                    v.wait_ge(ss.sem, ss.n)

                blk(("vector", s_av))

        def emit_qkv(i, L, do_attn):
            def t_mm(te, i=i):
                ss.wait(te)
                sg.wait(te)
                sv.wait(te)
                for nb in range(3):
                    for k in range(KC):
                        off = 1 if k < 4 else 0
                        ins = te.matmul(
                            PQ[:, nb * 512:(nb + 1) * 512],
                            stripA[k][:, off + i * 128: off + (i + 1) * 128],
                            wvq(k)[:, nb * 512:(nb + 1) * 512],
                            start=(k == 0), stop=(k == KC - 1))
                stn.inc(ins)

            def s_rot(s, i=i):
                sv.wait(s)
                sd.inc(s.dma_start(out=rtt[:],
                                   in_=bass.AP(rotpk, i * 128 * 256,
                                               [[256, 128], [1, 256]])))

            def s_sqn(v, i=i):
                stn.wait(v)
                VI(v, v.tensor_copy(scr[:, 0:1024], PQ[:, 0:1024]))
                ss.inc(v.tensor_tensor(scr[:, 0:1024], scr[:, 0:1024],
                                       scr[:, 0:1024], ALU.mult))
                v.wait_ge(ss.sem, ss.n)


            def v_n(v, i=i):
                stn.wait(v)
                sd.wait(v)
                ss.wait(v)
                VI(v, v.tensor_tensor(rtt[:], rtt[:], qkp[:], ALU.mult))
                q3 = scr[:, 0:512].rearrange("p (h d) -> p h d", d=DH)
                k3 = scr[:, 512:1024].rearrange("p (h d) -> p h d", d=DH)
                VI(v, v.tensor_reduce(stt[:, 0:8], q3, AX.X, ALU.add))
                VI(v, v.tensor_reduce(stt[:, 8:16], k3, AX.X, ALU.add))

            def s_n(sc, i=i):
                sv.wait(sc)
                SI(sc, sc.activation(stt[:, 16:32], stt[:, 0:16], AF.Sqrt))

            def s_n_cp(v, i=i):
                stn.wait(v)
                ss.inc(v.tensor_copy(vts[:, (i + 4) * INNER:(i + 5) * INNER],
                                     PQ[:, 1024:1536]))
                v.wait_ge(ss.sem, ss.n)

            def v_rot(v, i=i):
                ss.wait(v)
                VI(v, v.tensor_scalar_max(stt[:, 16:32], stt[:, 16:32], 1e-12))
                VI(v, v.reciprocal(stt[:, 16:32], stt[:, 16:32]))
                for qk in range(2):
                    rb = stt[:, 16 + 8 * qk:24 + 8 * qk].unsqueeze(2) \
                        .broadcast_to([128, 8, 64])
                    p3 = PQ[:, qk * 512:(qk + 1) * 512] \
                        .rearrange("p (h d) -> p h d", d=DH)
                    t13 = scr[:, 0:512].rearrange("p (h d) -> p h d", d=DH)
                    VI(v, v.tensor_tensor(t13, p3, rb, ALU.mult))
                    cs_t = rtt[:, 128 * qk:128 * qk + 64].unsqueeze(1) \
                        .broadcast_to([128, 8, 64])
                    sn_t = rtt[:, 128 * qk + 64:128 * qk + 128].unsqueeze(1) \
                        .broadcast_to([128, 8, 64])
                    u3 = scr[:, 512:1024].rearrange("p (h d) -> p h d", d=DH)
                    VI(v, v.tensor_tensor(u3, t13, cs_t, ALU.mult))
                    w3 = scr[:, 1024:1536].rearrange("p (h d) -> p h d", d=DH)
                    VI(v, v.tensor_tensor(w3, rotread(scr[:, 0:512], 8, 64),
                                           sn_t, ALU.mult))
                    dst = tlb2[:, qk * 512:(qk + 1) * 512]
                    VI(v, v.tensor_tensor(dst, scr[:, 512:1024], scr[:, 1024:1536],
                                           ALU.add))

            def t_tr(te, i=i):
                sv.wait(te)
                ss.wait(te)
                for jj in range(4):
                    te.transpose(PTR[:, jj * 128:(jj + 1) * 128],
                                 tlb2[:, jj * 128:(jj + 1) * 128], sid[:])
                for jj in range(4):
                    ins = te.transpose(PTR[:, 512 + jj * 128:512 + (jj + 1) * 128],
                                       tlb2[:, 512 + jj * 128:512 + (jj + 1) * 128],
                                       sid[:])
                stn.inc(ins)

            def s_tr(v, i=i):
                stn.wait(v)
                ins = None
                for jj in range(4):
                    ins = v.tensor_copy(qsl(i)[:, jj * 128:(jj + 1) * 128],
                                        PTR[:, jj * 128:(jj + 1) * 128])
                for jj in range(4):
                    ins = v.tensor_copy(kts[jj][:, WIN + i * 128: WIN + (i + 1) * 128],
                                        PTR[:, 512 + jj * 128:512 + (jj + 1) * 128])
                ss.inc(ins)

            blk(("tensor", t_mm), ("sync", s_rot))
            blk(("vector", s_sqn))
            if L == 0 and i == 5:
                tap("sq5", lambda: scr[:, 0:1024], [128, 1024])
            blk(("vector", v_n))
            blk(("scalar", s_n), ("vector", s_n_cp))
            if L == 0 and i == 5:
                tap("rtt5", lambda: rtt[:], [128, 256])
                tap("rn5", lambda: stt[:, 0:48], [128, 48])
            blk(("vector", v_rot))
            if L == 0 and i == 5:
                tap("t13", lambda: scr[:, 0:1536], [128, 1536])
                tap("qrot5", lambda: tlb2[:], [128, 1024], BF16)
            blk(("tensor", t_tr), ("vector", s_tr))
            if L == 0 and i == 5:
                tap("q5", lambda: qtt[:], [128, 512], BF16)
                tap("k5", lambda: kts[0][:, WIN + 5 * 128:WIN + 6 * 128], [128, 128], BF16)
                tap("v5", lambda: vts[:, 9 * INNER:10 * INNER], [128, 512], BF16)
            if do_attn:
                emit_attn(i, L)
                if L == 0 and i == 5:
                    tap("p5", lambda: pbuf[:, SPAN:2 * SPAN], [128, SPAN], BF16)
                    tap("o5", lambda: stripA[0][:, 5 * 128:6 * 128], [128, 128], BF16)

        for L in range(2):
            def b0_g(g, L=L):
                stn.wait(g)
                sg.inc(g.dma_start(
                    out=wv(0, 12288).rearrange("p (k c) -> p k c", c=1536),
                    in_=bass.AP(wqkv, L * D * 3 * INNER,
                                [[3 * INNER, 128], [128 * 3 * INNER, KC],
                                 [1, 3 * INNER]])))

            def b0_s(s, L=L):
                sv.wait(s)
                ss.wait(s)
                sd.inc(s.dma_start(out=lngF[:],
                                   in_=bass.AP(f_ln_g, L * D, [[1, 128], [128, KC]]), allow_slow_non_contiguous=True))
                sd.inc(s.dma_start(out=lnbF[:],
                                   in_=bass.AP(f_ln_b, L * D, [[1, 128], [128, KC]]), allow_slow_non_contiguous=True))
                sd.inc(s.dma_start(out=qkp[:],
                                   in_=bass.AP(qkpat, L * 128 * 256,
                                               [[256, 128], [1, 256]])))

            blk(("gpsimd", b0_g), ("sync", b0_s))

            for i in range(NT):
                def b1_v(v, i=i):
                    ss.wait(v)
                    VI(v, v.tensor_copy(tlb[:], cur[:, i * D:(i + 1) * D]))
                blk(("vector", b1_v))
                emit_tile_transpose(stripA, 1, i)

            for i in range(1, NT):
                emit_qkv(i, L, do_attn=(i >= 5))

            def b3_pack(s, L=L):
                ss.wait(s)
                off = 0
                for j in range(4):
                    sd.inc(s.dma_start(
                        out=bass.AP(agk_in[L], off, [[512, 128], [1, 512]]),
                        in_=kts[j][:, WIN + T - 512: WIN + T]))
                    off += 128 * 512
                for jt in range(4):
                    sd.inc(s.dma_start(
                        out=bass.AP(agk_in[L], off, [[512, 128], [1, 512]]),
                        in_=vts[:, (NT + jt) * INNER:(NT + 1 + jt) * INNER]))
                    off += 128 * 512
                for k in range(KC):
                    sd.inc(s.dma_start(
                        out=bass.AP(agk_in[L], off + k * 128, [[1, 1], [1, 128]]),
                        in_=stripA[k][:, T:T + 1]))

            def b3_cc(g, L=L):
                sd.wait(g)
                scc.inc(g.collective_compute(
                    "AllGather", ALU.bypass, replica_groups=[list(range(8))],
                    ins=[agk_in[L].ap()], outs=[agk_out[L].ap()]))

            def b3_recv(g, L=L):
                scc.wait(g)
                for jj in range(9):
                    with g.register() as gj:
                        g.reg_load(gj, soffk[0:1, 0:1])
                        g.reg_add(gj, gj, jj * 128 * 512)
                        if jj < 4:
                            src = bass.AP(agk_out[L], gj, [[512, 128], [1, 512]])
                            ins = g.dma_start(out=kts[jj][:, 0:WIN], in_=src)
                        elif jj < 8:
                            src = bass.AP(agk_out[L], gj, [[512, 128], [1, 512]])
                            ins = g.dma_start(
                                out=vts[:, (jj - 4) * INNER:(jj - 3) * INNER], in_=src)
                        else:
                            src = bass.AP(agk_out[L], gj, [[1, 128], [128, KC]])
                            ins = g.dma_start(out=pcol[:], in_=src,
                                              allow_slow_non_contiguous=True)
                        sg.inc(ins)

            blk(("sync", b3_pack))
            flushb()
            blk(("gpsimd", b3_cc))
            flushb()
            blk(("gpsimd", b3_recv))

            def b3_lead(v):
                sg.wait(v)
                for k in range(KC):
                    VI(v, v.tensor_scalar_mul(stripA[k][:, 0:1], pcol[:, k:k + 1],
                                               sqz[:]))

            blk(("vector", b3_lead))

            emit_qkv(0, L, do_attn=True)
            if L == 0:
                tap("q0t", lambda: qtt5[:, 0:512], [128, 512], BF16)
                tap("k0t", lambda: kts[0][:, 512:640], [128, 128], BF16)
                tap("o0t", lambda: stripA[0][:, 0:128], [128, 128], BF16)
                tap("p0", lambda: pbuf[:, 0:SPAN], [128, SPAN], BF16)
                tap("kh0", lambda: kts[0][:, 0:512], [128, 512], BF16)
                tap("vh0", lambda: vts[:, 0:512], [128, 512], BF16)
            for i in range(1, 5):
                emit_attn(i, L)

            def b5_g(g, L=L):
                stn.wait(g)
                sg.inc(g.dma_start(
                    out=wv(0, 4096).rearrange("p (k c) -> p k c", c=D),
                    in_=bass.AP(wo, L * INNER * D, [[D, 128], [128 * D, 4], [1, D]])))

            blk(("gpsimd", b5_g))

            for i in range(NT - 1, -1, -1):
                def b5_t(te, i=i):
                    ss.wait(te)
                    sg.wait(te)
                    sv.wait(te)
                    for nb in range(2):
                        for j in range(4):
                            ins = te.matmul(PQ[:, nb * 512:(nb + 1) * 512],
                                            stripA[j][:, i * 128:(i + 1) * 128],
                                            wvo(j)[:, nb * 512:(nb + 1) * 512],
                                            start=(j == 0), stop=(j == 3))
                    stn.inc(ins)

                def b5_v(v, i=i):
                    stn.wait(v)
                    VI(v, v.tensor_tensor(cur[:, i * D:(i + 1) * D],
                                           cur[:, i * D:(i + 1) * D],
                                           PQ[:, 0:1024], ALU.add))
                    VI(v, v.tensor_copy(tlb[:], cur[:, i * D:(i + 1) * D]))

                blk(("tensor", b5_t), ("vector", b5_v))
                emit_tile_transpose(stripA, 1, i)

            tap(f"x2T_{L}", lambda: stripA[0][:, 1:1 + T], [128, T], BF16)
            tap(f"x2cur_{L}", lambda: cur[:], [128, NT * D])

            def b6_pack(s, L=L):
                ss.wait(s)
                for k in range(KC):
                    sd.inc(s.dma_start(
                        out=bass.AP(agc_in[L], k * 128, [[1, 1], [1, 128]]),
                        in_=stripA[k][:, T:T + 1]))

            def b6_cc(g, L=L):
                sd.wait(g)
                scc.inc(g.collective_compute(
                    "AllGather", ALU.bypass, replica_groups=[list(range(8))],
                    ins=[agc_in[L].ap()], outs=[agc_out[L].ap()]))

            def b6_recv(g, L=L):
                scc.wait(g)
                with g.register() as gc:
                    g.reg_load(gc, soffc[0:1, 0:1])
                    src = bass.AP(agc_out[L], gc, [[1, 128], [128, KC]])
                    sg.inc(g.dma_start(out=pcol[:], in_=src,
                                       allow_slow_non_contiguous=True))

            blk(("sync", b6_pack))
            flushb()
            blk(("gpsimd", b6_cc))
            flushb()
            blk(("gpsimd", b6_recv))

            def b6_lead(v):
                sg.wait(v)
                for k in range(KC):
                    VI(v, v.tensor_scalar_mul(stripA[k][:, 0:1], pcol[:, k:k + 1],
                                               sqz[:]))

            blk(("vector", b6_lead))

            def shifted(k, ap2, c0, n):
                off = 1 if k < 4 else 0
                return ap2[:, off + c0: off + c0 + n]

            STR = [PQ[0:1, 0:512], PQ[0:1, 512:1024], PQ[0:1, 1024:1536],
                   PS0[0:1, 0:512]]
            def b7_st1(te):
                ss.wait(te)
                sv.wait(te)
                for k in range(KC):
                    for cb in range(4):
                        ins = te.matmul(STR[cb], so_c[:],
                                        shifted(k, stripA[k], cb * 512, 512),
                                        start=(k == 0), stop=(k == KC - 1))
                stn.inc(ins)

            def b7_ev1(v):
                stn.wait(v)
                for cb in range(4):
                    VI(v, v.tensor_copy(scr[0:1, cb * 512:(cb + 1) * 512], STR[cb]))

            blk(("tensor", b7_st1))
            blk(("vector", b7_ev1))

            for k in range(KC):
                def b7_sq(v, k=k):
                    stn.wait(v)
                    ss.inc(v.tensor_tensor(sqs[:, 0:1 + T], stripA[k][:, 0:1 + T],
                                           stripA[k][:, 0:1 + T], ALU.mult))
                    v.wait_ge(ss.sem, ss.n)

                def b7_st2(te, k=k):
                    ss.wait(te)
                    if k == 0:
                        sv.wait(te)
                    for cb in range(4):
                        ins = te.matmul(STR[cb], so_c[:],
                                        shifted(k, sqs, cb * 512, 512),
                                        start=(k == 0), stop=(k == KC - 1))
                    stn.inc(ins)

                blk(("vector", b7_sq), ("tensor", b7_st2))

            def b7_ev2(v):
                stn.wait(v)
                spots = [cp2[0:1, 0:512], cp2[0:1, 512:1024],
                         cp2[0:1, 1024:1536], arena[0:1, 19337:19849]]
                for cb in range(4):
                    VI(v, v.tensor_copy(spots[cb], STR[cb]))

            blk(("vector", b7_ev2))

            spots = [cp2[0:1, 0:512], cp2[0:1, 512:1024],
                     cp2[0:1, 1024:1536], arena[0:1, 19337:19849]]
            t_mu = arena[0:1, 19849:20361]
            t_m2 = arena[0:1, 20361:20873]
            t_ms = arena[0:1, 20873:21385]
            for cb in range(4):
                def b7_rm1(v, cb=cb):
                    if cb == 0:
                        stn.wait(v)
                    ss.wait(v)
                    sd.wait(v)
                    s1b = scr[0:1, cb * 512:(cb + 1) * 512]
                    s2b = spots[cb]
                    VI(v, v.tensor_scalar_mul(t_mu, s1b, 1.0 / D))
                    VI(v, v.tensor_scalar_mul(s2b, s2b, 1.0 / D))
                    VI(v, v.tensor_tensor(t_m2, t_mu, t_mu, ALU.mult))
                    VI(v, v.tensor_tensor(s2b, s2b, t_m2, ALU.subtract))

                def b7_rm2(sc, cb=cb):
                    sv.wait(sc)
                    SI(sc, sc.activation(spots[cb], spots[cb], AF.Sqrt,
                                         bias=seps[0:1, :]))

                def b7_rm3(v, cb=cb):
                    ss.wait(v)
                    VI(v, v.reciprocal(spots[cb], spots[cb]))
                    VI(v, v.tensor_tensor(t_ms, t_mu, spots[cb], ALU.mult))

                def b7_rm4(s, cb=cb):
                    sv.wait(s)
                    sd.inc(s.dma_start(out=bass.AP(rwsp, cb * 512, [[512, 1], [1, 512]]),
                                       in_=spots[cb]))
                    sd.inc(s.dma_start(out=bass.AP(rwsp, T + cb * 512,
                                                   [[512, 1], [1, 512]]),
                                       in_=t_ms))

                blk(("vector", b7_rm1))
                blk(("scalar", b7_rm2))
                blk(("vector", b7_rm3), ("sync", b7_rm4))

            for hf in (1, 0):
                def b7_bc(s, hf=hf):
                    sv.wait(s)
                    ss.wait(s)
                    s.wait_ge(sd.sem, sd.n)
                    sd.inc(s.dma_start(out=bcr1[:],
                                       in_=bass.AP(rwsp, hf * 1024,
                                                   [[0, 128], [1, 1024]])))
                    sd.inc(s.dma_start(out=bcr2[:],
                                       in_=bass.AP(rwsp, T + hf * 1024,
                                                   [[0, 128], [1, 1024]])))

                blk(("sync", b7_bc))
                for k in range(KC):
                    def b7_apv(v, k=k, hf=hf):
                        sd.wait(v)
                        ss.wait(v)
                        VI(v, v.tensor_tensor(bscr,
                                               shifted(k, stripA[k], hf * 1024, 1024),
                                               bcr1[:], ALU.mult))
                        VI(v, v.tensor_tensor(bscr, bscr, bcr2[:], ALU.subtract))

                    def b7_aps(v, k=k, hf=hf):
                        VI(v, v.tensor_scalar(
                            stripA[k][:, 1 + hf * 1024: 1 + (hf + 1) * 1024],
                            bscr, lngF[:, k:k + 1], lnbF[:, k:k + 1],
                            ALU.mult, ALU.add))

                    blk(("vector", b7_apv))
                    blk(("vector", b7_aps))

            tap(f"ln2T_{L}", lambda: stripA[0][:, 1:1 + T], [128, T], BF16)

            def b7_bg(g, L=L):
                sg.inc(g.dma_start(out=sb2r[:],
                                   in_=bass.AP(b2, L * D, [[1, 1], [1, D]])))

            blk(("gpsimd", b7_bg))

            def emit_wg(g_i, L=L):
                gw, g0 = W1GW[g_i], W1G0[g_i]
                bofs = 0
                sofs = 0

                def b7_wg(g, L=L, gw=gw, g0=g0, bofs=bofs, sofs=sofs):
                    stn.wait(g)
                    for half in range(2):
                        o3 = wv(bofs + half * KC * WSL,
                                bofs + (half + 1) * KC * WSL) \
                            .rearrange("p (k c) -> p k c", c=WSL)[:, :, 0:gw]
                        sg.inc(g.dma_start(
                            out=o3,
                            in_=bass.AP(w1, L * D * 2 * HID + half * HID + g0,
                                        [[2 * HID, 128], [128 * 2 * HID, KC],
                                         [1, gw]])))
                        sg.inc(g.dma_start(
                            out=sb1g[:, sofs + half * WSL:sofs + half * WSL + gw],
                            in_=bass.AP(b1, L * 2 * HID + half * HID + g0,
                                        [[1, 1], [1, gw]])))

                blk(("gpsimd", b7_wg))
                flushb()
                return sg.n

            wgn = {}
            w2n = {}
            for g_i in range(NW1G):
                gw, g0 = W1GW[g_i], W1G0[g_i]
                nchk = (gw + 127) // 128
                bofs = 0
                sofs = 0
                wgn[g_i] = emit_wg(g_i)
                if g_i == NW1G - 1:
                    def b7_w2g0(g, L=L):
                        sg.inc(g.dma_start(
                            out=w2h[:].rearrange("p (k c) -> p k c", c=512),
                            in_=bass.AP(w2p, L * HIDP * D,
                                        [[D, 128], [128 * D, KH], [1, 512]])))

                    blk(("gpsimd", b7_w2g0))
                    flushb()
                    w2n[0] = sg.n

                def wv1(k, half, bofs=bofs):
                    return wv(bofs + (half * KC + k) * WSL,
                              bofs + (half * KC + k + 1) * WSL)

                for i in range(NT):
                    def b7_mm(te, i=i, gw=gw, g_i=g_i, wv1=wv1, sofs=sofs,
                              sgn=wgn[g_i]):
                        te.wait_ge(sg.sem, sgn)
                        ss.wait(te)
                        sv.wait(te)
                        for ag in range(2):
                            pd = [PS0, PS1][ag]
                            for k in range(KC):
                                te.matmul(pd[:, 0:gw],
                                          stripA[k][:, 1 + i * 128: 1 + (i + 1) * 128],
                                          wv1(k, ag)[:, 0:gw],
                                          start=(k == 0), stop=False)
                            ins = te.matmul(pd[:, 0:gw], so_r[:],
                                            sb1g[:, sofs + ag * WSL:
                                                 sofs + ag * WSL + gw],
                                            start=False, stop=True)
                        stn.inc(ins)

                    def b7_ge(sc, i=i, gw=gw):
                        stn.wait(sc)
                        SI(sc, sc.activation(tlb[:, 0:gw], PS1[:, 0:gw], AF.Gelu))

                    def b7_mu(v, i=i, gw=gw):
                        ss.wait(v)
                        VI(v, v.tensor_tensor(tlb2[:, 0:gw], PS0[:, 0:gw],
                                               tlb[:, 0:gw], ALU.mult))

                    def b7_tr(te, i=i, gw=gw, nchk=nchk):
                        sv.wait(te)
                        sd.wait(te)
                        ss.wait(te)
                        for c in range(nchk):
                            c0c, c1c = c * 128, min((c + 1) * 128, gw)
                            ins = te.transpose(PTR[0:c1c - c0c, c * 128:c * 128 + 128],
                                               tlb2[:, c0c:c1c], sid[:])
                        stn.inc(ins)

                    def b7_ev(v, i=i, gw=gw, nchk=nchk):
                        stn.wait(v)
                        ins = None
                        for c in range(nchk):
                            c0c, c1c = c * 128, min((c + 1) * 128, gw)
                            ins = v.tensor_copy(hgt[0:c1c - c0c, c * 128:c * 128 + 128],
                                                PTR[0:c1c - c0c, c * 128:c * 128 + 128])
                        ss.inc(ins)

                    def b7_sp(s, i=i, g_i=g_i, gw=gw, nchk=nchk, g0=g0):
                        ss.wait(s)
                        ch0 = g0 // 128
                        for c in range(nchk):
                            c0c, c1c = c * 128, min((c + 1) * 128, gw)
                            sd.inc(s.dma_start(
                                out=bass.AP(hgesp, (i * KH + ch0 + c) * 128 * 128,
                                            [[128, c1c - c0c], [1, 128]]),
                                in_=hgt[0:c1c - c0c, c * 128:c * 128 + 128]))
                        if g_i == NW1G - 1:
                            sd.inc(s.dma_start(
                                out=bass.AP(hgesp, (i * KH + 21) * 128 * 128 + 42 * 128,
                                            [[128, 86], [1, 128]]),
                                in_=zzb[0:86, 0:128]))

                    blk(("tensor", b7_mm))
                    blk(("scalar", b7_ge))
                    blk(("vector", b7_mu))
                    blk(("tensor", b7_tr), ("vector", b7_ev), ("sync", b7_sp))


            def b7_w2g1(g, L=L):
                stn.wait(g)
                sg.inc(g.dma_start(
                    out=wv(0, KH * 512).rearrange("p (k c) -> p k c", c=512),
                    in_=bass.AP(w2p, L * HIDP * D + 512,
                                [[D, 128], [128 * D, KH], [1, 512]])))

            blk(("gpsimd", b7_w2g1))
            flushb()
            w2n[1] = sg.n

            for hf in range(2):
                wsrc = w2h if hf == 0 else wv(0, KH * 512)
                for i in range(NT):
                    def b7_ld(s, i=i):
                        stn.wait(s)
                        s.wait_ge(sd.sem, sd.n)
                        sd.inc(s.dma_start(
                            out=hgt[:].rearrange("p (k c) -> p k c", c=128),
                            in_=bass.AP(hgesp, i * KH * 128 * 128,
                                        [[128, 128], [128 * 128, KH], [1, 128]])))

                    def b7_w2mm(te, i=i, hf=hf, wsrc=wsrc, sgn=w2n[hf]):
                        sd.wait(te)
                        te.wait_ge(sg.sem, sgn)
                        sv.wait(te)
                        for k in range(KH):
                            te.matmul(PQ[:, 0:512], hgt[:, k * 128:(k + 1) * 128],
                                      wsrc[:, k * 512:(k + 1) * 512],
                                      start=(k == 0), stop=False)
                        ins = te.matmul(PQ[:, 0:512], so_r[:],
                                        sb2r[:, hf * 512:(hf + 1) * 512],
                                        start=False, stop=True)
                        stn.inc(ins)

                    def b7_res(v, i=i, hf=hf):
                        stn.wait(v)
                        VI(v, v.tensor_tensor(
                            cur[:, i * D + hf * 512: i * D + (hf + 1) * 512],
                            cur[:, i * D + hf * 512: i * D + (hf + 1) * 512],
                            PQ[:, 0:512], ALU.add))

                    blk(("sync", b7_ld))
                    blk(("tensor", b7_w2mm), ("vector", b7_res))

            tap(f"x3_{L}", lambda: cur[:, 0:D], [128, D])

        emit_ln_stats()
        for i in range(NT):
            def c_ld(s, i=i):
                sv.wait(s)
                ss.wait(s)
                s.wait_ge(sd.sem, sd.n)
                sd.inc(s.dma_start(out=scr[:, 1024:2048],
                                   in_=bass.AP(x1sp, i * 128 * D, [[D, 128], [1, D]])))

            def c_v2(v, i=i):
                sd.wait(v)
                VI(v, v.tensor_scalar(cur[:, i * D:(i + 1) * D],
                                       cur[:, i * D:(i + 1) * D],
                                       stt[:, 32 + i:33 + i], stt[:, 48 + i:49 + i],
                                       ALU.subtract, ALU.mult))
                VI(v, v.tensor_tensor(cur[:, i * D:(i + 1) * D],
                                       cur[:, i * D:(i + 1) * D],
                                       scr[:, 1024:2048], ALU.add))

            def c_st(s, i=i):
                sv.wait(s)
                fin.inc(s.dma_start(out=bass.AP(out, i * 128 * D, [[D, 128], [1, D]]),
                                    in_=cur[:, i * D:(i + 1) * D]))

            blk(("sync", c_ld))
            blk(("vector", c_v2), ("sync", c_st))

        def c_fin(s):
            fin.wait(s)

        blk(("sync", c_fin))
        flushb()

    import os
    if os.environ.get('KPRINT'):
        print('sem counts:', {'sd': sd.n, 'sg': sg.n, 'sv': sv.n, 'ss': ss.n,
              'stn': stn.n, 'sq': sq.n, 'scc': scc.n, 'fin': fin.n})
    return nc, taps


def _host_tables():
    half = np.arange(0, DH, 2, dtype=np.float64)
    inv_freq = 1.0 / 10000.0 ** (half / DH)
    t = np.arange(S, dtype=np.float64)
    freqs = t[:, None] * inv_freq[None, :]
    freqs = np.concatenate([freqs, freqs], axis=-1)
    base = (half + 0.4 * DH) / (1.4 * DH)
    power = (t - S // 2) / float(WIN // 2)
    scale = base[None, :] ** power[:, None]
    scale = np.concatenate([scale, scale], axis=-1)
    cos, sin = np.cos(freqs), np.sin(freqs)
    sign = np.concatenate([-np.ones(32), np.ones(32)])
    csq = cos * scale * 8.0
    snq = sin * scale * 8.0 * sign[None, :]
    csk = cos / scale
    snk = sin / scale * sign[None, :]
    return np.concatenate([csq, snq, csk, snk], axis=1).astype(np.float32)


_ROT = None


def _masks(t0):
    m = np.zeros((NT, 128, SPAN), np.float32)
    r = np.arange(128)[None, :, None]
    jj = np.arange(SPAN)[None, None, :]
    i = np.arange(NT)[:, None, None]
    absk = t0 + i * 128 - WIN + jj
    allowed = (jj >= r) & (jj <= r + WIN) & (absk >= 0)
    m[~allowed] = -3.0e38
    return m.astype(ml_dtypes.bfloat16)


def make_inputs(inputs, nc_dbg=None):
    global _ROT
    if _ROT is None:
        _ROT = _host_tables()
    x = np.asarray(inputs["L0_attn_x"])
    base = {
        "w_inT": _f(np.asarray(inputs["L0_attn_W_in"]).T),
        "w_gateT": _f(np.asarray(inputs["L0_attn_W_gate"]).T),
        "a_ln_g": _f(inputs["L0_attn_ln_g"]),
        "a_ln_b": _f(inputs["L0_attn_ln_b"]),
        "bgate": _f(inputs["L0_attn_b_gate"]),
        "eigr": _f(inputs["L0_attn_eig_raw"]),
        "w_outT": _f(np.asarray(inputs["L0_attn_W_out"]).T),
        "wqkv": _f(inputs["L0_ffn_wqkv"]),
        "wo": _f(inputs["L0_ffn_wo"]),
        "w1": _f(inputs["L0_ffn_w1"]),
        "b1": _f(inputs["L0_ffn_b1"]),
        "w2p": _f(np.pad(np.asarray(inputs["L0_ffn_w2"], np.float32),
                         ((0, 0), (0, HIDP - HID), (0, 0)))),
        "b2": _f(inputs["L0_ffn_b2"]),
        "f_ln_g": _f(inputs["L0_ffn_ln_g"]),
        "f_ln_b": _f(inputs["L0_ffn_ln_b"]),
        "identb": _bfn(np.eye(128)),
        "onescol": _bfn(np.ones((128, 1))),
        "onesrow": _bfn(np.ones((1, 128))),
    }
    qs = np.asarray(inputs["L0_ffn_q_scale"], np.float32)
    ks = np.asarray(inputs["L0_ffn_k_scale"], np.float32)
    ri = np.concatenate([np.arange(32, 64), np.arange(0, 32)])
    pat = np.zeros((2, 128, 256), np.float32)
    for l in range(2):
        row = np.concatenate([qs[l], qs[l][ri], ks[l], ks[l][ri]])
        pat[l] = row[None, :]
    base["qkpat"] = pat
    estr = np.zeros((KC, 128, H), np.float32)
    for k in range(KC):
        for p in range(128):
            estr[k, p, (128 * k + p) // HD] = 1.0
    base["estrip"] = _bfn(estr)

    in_maps = []
    for c in range(8):
        b, q = c // 4, c % 4
        t0 = q * T
        im = dict(base)
        im["xin"] = _f(x[b, t0:t0 + T])
        im["rotpk"] = _ROT[t0:t0 + T].copy()
        im["maskt"] = _masks(t0)
        im["qzin"] = _f(np.full((128, 1), 0.0 if q == 0 else 1.0))
        prev = c - 1 if q > 0 else c
        im["poffk"] = np.array([[prev * AGKN]], np.int32)
        im["poffc"] = np.array([[prev * AGCN]], np.int32)
        sa = np.zeros((H, 8), np.float32)
        for sl in range(8):
            if (sl // 4 == b) and (sl % 4 < q):
                sa[:, sl] = 1.0
        im["selAd"] = sa
        im["selAid"] = 1.0 - sa
        im["selBd"] = np.repeat(sa, HD, axis=1)
        in_maps.append(im)
    return in_maps


def kernel(**inputs):
    nc, _ = build_program()
    in_maps = make_inputs(inputs)
    res = run_bass_kernel_spmd(nc, in_maps, core_ids=list(range(8)))
    outp = np.zeros((B, S, D), np.float32)
    for c in range(8):
        b, q = c // 4, c % 4
        outp[b, q * T:(q + 1) * T] = res.results[c]["out"]
    return outp
